# version 16
# baseline (speedup 1.0000x reference)
import contextlib
import numpy as np
import concourse.bass as bass
import concourse.mybir as mybir
from concourse.bass_utils import run_bass_kernel_spmd

F32 = mybir.dt.float32
BF16 = mybir.dt.bfloat16
AF = mybir.ActivationFunctionType
ALU = mybir.AluOpType

NCORE = 8
TOK = 2048
D = 2048
KB = 16
H = 5632
HBK = 44
MT = 16
QT = 512
NQ = 4
G3 = 11
NG3 = HBK // G3
EPS = 1e-5
EPS16 = 16.0 * EPS
WELEM = float(H * D)
SLICE = H * D // NCORE
GCH = 1408
NGC = SLICE // (128 * GCH)

_NC_CACHE = {}
_DEBUG_Y2 = False


class Emitter:
    def __init__(self, sems, counts=None):
        self.q = {k: [] for k in ("sync", "scalar", "vector", "tensor", "gpsimd")}
        self.n = dict(counts) if counts else {name: 0 for name in sems}
        self.sems = sems

    def emit(self, eng, fn, waits=(), incs=(), ms=False):
        waits = [(s, v) for (s, v) in waits if v is not None and v > 0]
        incs = list(incs)
        after = None
        for s, k in incs:
            self.n[s] += k
            after = self.n[s]
        self.q[eng].append((fn, waits, incs))
        return after if ms else None

    def replay(self, eng_name, eng):
        for fn, waits, incs in self.q[eng_name]:
            for s, v in waits:
                eng.wait_ge(self.sems[s], v)
            ins = fn(eng)
            for s, k in incs:
                ins.then_inc(self.sems[s], k)

    def run_block(self, nc):
        with nc.Block() as block:
            @block.sync
            def _(e):
                self.replay("sync", e)

            @block.scalar
            def _(e):
                self.replay("scalar", e)

            @block.vector
            def _(e):
                self.replay("vector", e)

            @block.tensor
            def _(e):
                self.replay("tensor", e)

            @block.gpsimd
            def _(e):
                self.replay("gpsimd", e)


def build_program():
    nc = bass.Bass("TRN2", target_bir_lowering=False, debug=False)

    x_d = nc.dram_tensor("x", [TOK, D], F32, kind="ExternalInput").ap()
    w0b_d = nc.dram_tensor("w0b", [128, D], F32, kind="ExternalInput").ap()
    w1b_d = nc.dram_tensor("w1b", [128, D], F32, kind="ExternalInput").ap()
    biasb_d = nc.dram_tensor("biasb", [128, D], F32, kind="ExternalInput").ap()
    dvrep_d = nc.dram_tensor("dvrep", [128, 2], F32, kind="ExternalInput").ap()
    onesrow_d = nc.dram_tensor("onesrow", [1, 128], F32, kind="ExternalInput").ap()
    onescol_d = nc.dram_tensor("onescol", [128, 1], F32, kind="ExternalInput").ap()
    epsh_d = nc.dram_tensor("epsh", [1, 1], F32, kind="ExternalInput").ap()
    wsl_d = nc.dram_tensor("wsl", [3, SLICE], F32, kind="ExternalInput").ap()
    w1f_d = nc.dram_tensor("w1f", [H, D], F32, kind="ExternalInput").ap()
    w2f_d = nc.dram_tensor("w2f", [H, D], F32, kind="ExternalInput").ap()
    w3f_d = nc.dram_tensor("w3f", [D, H], F32, kind="ExternalInput").ap()
    out_d = nc.dram_tensor("out", [TOK, D], F32, kind="ExternalOutput").ap()
    q1t_d = nc.dram_tensor("q1t", [H, D], BF16, kind="Internal").ap()
    q2t_d = nc.dram_tensor("q2t", [H, D], BF16, kind="Internal").ap()
    q3t_d = nc.dram_tensor("q3t", [H, D], BF16, kind="Internal").ap()
    h1t_d = nc.dram_tensor("h1t", [KB * 128, TOK], BF16, kind="Internal").ap()
    h1n_d = nc.dram_tensor("h1n", [TOK, D], BF16, kind="Internal").ap()
    ccin_d = nc.dram_tensor("ccin", [128, 4], F32, kind="Internal").ap()
    ccout_d = nc.dram_tensor("ccout", [128, 4], F32, kind="Internal").ap()
    dbgy2_d = nc.dram_tensor("dbgy2", [TOK, D], F32,
                             kind="ExternalOutput" if _DEBUG_Y2 else "Internal").ap()
    dbgth_d = nc.dram_tensor("dbgth", [128, 64], F32,
                             kind="ExternalOutput" if _DEBUG_Y2 else "Internal").ap()

    sem_names = (["cv", "cs", "ct", "cg", "sK", "sDBG", "sH1", "sCC", "sCC2"] +
                 [f"sX{i}" for i in range(2)] + [f"sG{i}" for i in range(2)] +
                 [f"sP{i}" for i in range(3)] + [f"sT{i}" for i in range(2)] +
                 [f"sS{i}" for i in range(2)] + [f"sHT{i}" for i in range(2)] +
                 [f"sHS{i}" for i in range(2)] + [f"sHN{i}" for i in range(2)] +
                 [f"sW1{i}" for i in range(2)] + [f"sW2{i}" for i in range(2)] +
                 [f"sW3{i}" for i in range(2)] +
                 [f"sHB{i}" for i in range(2)] + [f"sO{i}" for i in range(2)])
    stack = contextlib.ExitStack()
    sems = {nm: stack.enter_context(nc.semaphore(nm)) for nm in sem_names}

    with stack:
        thc = stack.enter_context(nc.sbuf_tensor("thc", [128, 8], F32))
        onesr = stack.enter_context(nc.sbuf_tensor("onesr", [1, 128], F32))
        onesc = stack.enter_context(nc.sbuf_tensor("onesc", [128, 1], F32))
        epshs = stack.enter_context(nc.sbuf_tensor("epshs", [1, 4], F32))
        gacc = stack.enter_context(nc.sbuf_tensor("gacc", [128, 4], F32))
        thsml = stack.enter_context(nc.sbuf_tensor("thsml", [1, 4], F32))
        stats = stack.enter_context(nc.sbuf_tensor("stats", [128, 80], F32))
        st2 = stack.enter_context(nc.sbuf_tensor("st2", [128, 8], F32))
        h1Th = stack.enter_context(nc.sbuf_tensor("h1Th", [128, KB * QT], BF16))
        w1t = stack.enter_context(nc.sbuf_tensor("w1t", [128, 2 * D], BF16))
        w2t = stack.enter_context(nc.sbuf_tensor("w2t", [128, 2 * D], BF16))
        sgb = stack.enter_context(nc.sbuf_tensor("sgb", [128, 2 * QT], BF16))
        hTq = stack.enter_context(nc.sbuf_tensor("hTq", [128, HBK * QT], BF16))
        prep_st = contextlib.ExitStack()
        pin = prep_st.enter_context(nc.sbuf_tensor("pin", [128, 3 * D], F32))
        s1b = prep_st.enter_context(nc.sbuf_tensor("s1b", [128, 2 * D], BF16))
        v2b = prep_st.enter_context(nc.sbuf_tensor("v2b", [128, 2 * D], BF16))
        qnb = prep_st.enter_context(nc.sbuf_tensor("qnb", [128, 2 * D], BF16))
        qtb = prep_st.enter_context(nc.sbuf_tensor("qtb", [128, 2 * D], BF16))
        attn_st = contextlib.ExitStack()
        xin = attn_st.enter_context(nc.sbuf_tensor("xin", [128, D], F32))
        sA = attn_st.enter_context(nc.sbuf_tensor("sA", [128, D], F32))
        sB = attn_st.enter_context(nc.sbuf_tensor("sB", [128, D], F32))
        sC = attn_st.enter_context(nc.sbuf_tensor("sC", [128, D], F32))
        ubf = attn_st.enter_context(nc.sbuf_tensor("ubf", [128, 2 * D], BF16))
        h1nb = attn_st.enter_context(nc.sbuf_tensor("h1nb", [128, 2 * D], BF16))
        tbuf = attn_st.enter_context(nc.sbuf_tensor("tbuf", [128, D], BF16))
        w0bs = attn_st.enter_context(nc.sbuf_tensor("w0bs", [128, D], BF16))
        w1bs = attn_st.enter_context(nc.sbuf_tensor("w1bs", [128, D], BF16))
        biasbs = attn_st.enter_context(nc.sbuf_tensor("biasbs", [128, D], BF16))
        dvreps = attn_st.enter_context(nc.sbuf_tensor("dvreps", [128, 2], F32))
        dvc = attn_st.enter_context(nc.sbuf_tensor("dvc", [128, 2], F32))
        gb = attn_st.enter_context(nc.sbuf_tensor("gb", [128, GCH], F32))
        gjunk = attn_st.enter_context(nc.sbuf_tensor("gjunk", [128, GCH], BF16))
        gch = attn_st.enter_context(nc.sbuf_tensor("gch", [128, 32], F32))
        p_th = stack.enter_context(nc.psum_tensor("p_th", [1, 4], F32))
        p_bc = stack.enter_context(nc.psum_tensor("p_bc", [128, 4], F32))
        px1 = stack.enter_context(nc.psum_tensor("px1", [128, 2 * QT], F32))
        px2 = stack.enter_context(nc.psum_tensor("px2", [128, 2 * QT], F32))
        pyA = stack.enter_context(nc.psum_tensor("pyA", [128, 512], F32))
        pyB = stack.enter_context(nc.psum_tensor("pyB", [128, 512], F32))

        E = Emitter(sems)

        d_rows = None
        for dst, src in ((dvreps[:], dvrep_d), (onesr[:], onesrow_d),
                         (onesc[:], onescol_d), (epshs[0:1, 0:1], epsh_d)):
            E.emit("scalar",
                   lambda e, dst=dst, src=src: e.dma_start(out=dst, in_=src),
                   incs=[("sK", 16)])
        for dst, src in ((w0bs[:], w0b_d), (w1bs[:], w1b_d), (biasbs[:], biasb_d)):
            d_rows = E.emit("gpsimd",
                            lambda e, dst=dst, src=src: e.dma_start(out=dst, in_=src),
                            incs=[("sK", 16)], ms=True)
        ms_dvc = E.emit("vector",
               lambda e: e.tensor_scalar(dvc[:], dvreps[:], 0.9, 1.0, ALU.max, ALU.min),
               waits=[("sK", d_rows)], incs=[("cv", 1)], ms=True)
        pstr = dvc[:].ap[0][0]
        dv0_b = bass.AP(dvc, 0, [[pstr, 128], [0, D]])
        dv1_b = bass.AP(dvc, 1, [[pstr, 128], [0, D]])

        flsl = wsl_d.rearrange("a b -> (a b)")
        NCHUNK = 3 * NGC
        ms_gab = [None] * NCHUNK
        for gidx in range(NCHUNK):
            slot = gidx % 2
            w_g = [("cs", ms_gab[gidx - 1])] if gidx >= 1 else []
            msg = E.emit("sync",
                         lambda e, gidx=gidx: e.dma_start(
                             out=gb[:],
                             in_=flsl[gidx * 128 * GCH:(gidx + 1) * 128 * GCH]
                             .rearrange("(p n) -> p n", p=128)),
                         waits=w_g, incs=[(f"sG{slot}", 16)], ms=True)
            ms_gab[gidx] = E.emit("scalar",
                                  lambda e, gidx=gidx: e.activation(
                                      gjunk[:], gb[:],
                                      AF.Abs, accum_out=gch[:, gidx:gidx + 1]),
                                  waits=[(f"sG{slot}", msg)],
                                  incs=[("cs", 1)], ms=True)
        mgs = None
        for wi in range(3):
            mgs = E.emit("vector",
                         lambda e, wi=wi: e.reduce_sum(
                             gacc[:, wi:wi + 1], gch[:, wi * NGC:(wi + 1) * NGC],
                             axis=mybir.AxisListType.X),
                         waits=[("cs", ms_gab[NCHUNK - 1])] if wi == 0 else [],
                         incs=[("cv", 1)], ms=True)
        ms_ci = E.emit("sync",
                       lambda e: e.dma_start(out=ccin_d, in_=gacc[:]),
                       waits=[("cv", mgs)], incs=[("sCC", 16)], ms=True)
        ms_cc = E.emit("gpsimd",
                       lambda e: e.collective_compute(
                           "AllReduce", ALU.add,
                           [list(range(NCORE))],
                           [ccin_d.opt()], [ccout_d.opt()]),
                       waits=[("sCC", ms_ci)], incs=[("sCC2", 1)], ms=True)
        ms_co = E.emit("sync",
                       lambda e: e.dma_start(out=gacc[:], in_=ccout_d),
                       waits=[("sCC2", ms_cc)], incs=[("sCC", 16)], ms=True)
        ms_thp = None
        for wi in range(3):
            ms_thp = E.emit(
                "tensor",
                lambda e, wi=wi: e.matmul(p_th[0:1, wi:wi + 1], gacc[:, wi:wi + 1],
                                          onesc[:], start=True, stop=True),
                waits=[("sCC", ms_co), ("sK", d_rows)], incs=[("ct", 1)], ms=True)
        ms_thr = E.emit(
            "vector",
            lambda e: e.tensor_scalar(thsml[0:1, 0:3], p_th[0:1, 0:3],
                                      0.5 / WELEM, 0.5e-8, ALU.mult, ALU.add),
            waits=[("ct", ms_thp)], incs=[("cv", 1)], ms=True)
        ms_thb = E.emit(
            "tensor",
            lambda e: e.matmul(p_bc[:, 0:3], onesr[:], thsml[0:1, 0:3],
                               start=True, stop=True),
            waits=[("cv", ms_thr)], incs=[("ct", 1)], ms=True)
        mcp = E.emit("vector",
                     lambda e: e.tensor_copy(thc[:, 0:3], p_bc[:, 0:3]),
                     waits=[("ct", ms_thb)], incs=[("cv", 1)], ms=True)
        ms_thall = E.emit(
            "vector",
            lambda e: e.tensor_scalar(thc[:, 4:7], thc[:, 0:3], -1.0, None, ALU.mult),
            waits=[("cv", mcp)], incs=[("cv", 1)], ms=True)
        ms_th = [ms_thall, ms_thall, ms_thall]
        if _DEBUG_Y2:
            E.emit("sync", lambda e: e.dma_start(out=dbgth_d[:, 0:8], in_=thc[:]),
                   waits=[("cv", ms_thall)], incs=[("sDBG", 16)])
            E.emit("sync", lambda e: e.dma_start(out=dbgth_d[:, 8:40], in_=gch[:]),
                   waits=[("cv", ms_thall)], incs=[("sDBG", 16)])
            E.emit("sync", lambda e: e.dma_start(out=dbgth_d[:, 40:44], in_=gacc[:]),
                   waits=[("cv", ms_thall)], incs=[("sDBG", 16)])

        ms_x = [None] * MT
        ms_xfree = [None] * MT
        ms_norm = [None] * MT
        ms_h1free = [None] * MT
        ms_h1tp = [None] * MT
        ms_var = [None] * MT
        ms_sqrt = [None] * MT
        ms_uc = [None] * MT
        ms_vadd = [None] * MT
        ms_tfree = [None] * MT

        def emit_x_dma(m):
            if m >= MT:
                return
            waits = [("cg", ms_xfree[m - 1])] if m >= 1 else []
            ms_x[m] = E.emit(
                "gpsimd",
                lambda e, m=m: e.dma_start(
                    out=xin[:],
                    in_=x_d[m * 128:(m + 1) * 128, :]),
                waits=waits, incs=[(f"sX{m % 2}", 16)], ms=True)

        def attn_v1(m):
            xm = xin[:]
            ub = ubf[:, (m % 2) * D:(m % 2) * D + D]
            waits = [(f"sX{m % 2}", ms_x[m]), ("cv", ms_dvc), ("sK", d_rows)]
            if m >= 1:
                waits.append(("cv", ms_uc[m - 1]))
            mg_u0 = E.emit("gpsimd", lambda e: e.tensor_mul(sA[:], xm, w0bs[:]),
                           waits=waits, incs=[("cg", 1)], ms=True)
            msc_r = E.emit("vector",
                   lambda e: e.tensor_tensor_scan(sB[:], dv0_b, sA[:], 0.0,
                                                  ALU.mult, ALU.add),
                   waits=[("cg", mg_u0)], incs=[("cv", 1)], ms=True)
            w_c = [("cg", ms_tfree[m - 1])] if m >= 1 else []
            msc_c = E.emit("vector",
                   lambda e: e.tensor_tensor_scan(sC[:], dv1_b, xm, 0.0,
                                                  ALU.mult, ALU.add),
                   waits=w_c, incs=[("cv", 1)], ms=True)
            mg_t = E.emit("gpsimd", lambda e: e.tensor_mul(sC[:], sC[:], w1bs[:]),
                          waits=[("cv", msc_c)], incs=[("cg", 1)], ms=True)
            ms_tfree[m] = mg_t
            mg_xb = E.emit("gpsimd", lambda e: e.tensor_add(sA[:], xm, biasbs[:]),
                           waits=[("cv", msc_r)], incs=[("cg", 1)], ms=True)
            ms_xfree[m] = mg_xb
            ms_vadd[m] = E.emit("vector",
                   lambda e: e.scalar_tensor_tensor(sC[:], sB[:], 1.0, sC[:],
                                                    ALU.bypass, ALU.add),
                   waits=[("cg", mg_t)], incs=[("cv", 1)], ms=True)
            wu = [("cv", ms_norm[m - 2])] if m >= 2 else []
            ms_uc[m] = E.emit("vector",
                   lambda e: e.scalar_tensor_tensor(ub, sC[:], 1.0, sA[:],
                                                    ALU.bypass, ALU.add,
                                                    accum_out=stats[:, 0:1]),
                   waits=wu + [("cg", mg_xb)], incs=[("cv", 1)], ms=True)
            mq = E.emit("vector",
                   lambda e: e.scalar_tensor_tensor(sB[:], ub, 1.0, ub,
                                                    ALU.bypass, ALU.mult,
                                                    accum_out=stats[:, 1:2]),
                   incs=[("cv", 1)], ms=True)
            o = 16 + m * 4
            mq = E.emit("vector",
                   lambda e, o=o: e.tensor_scalar(stats[:, o:o + 1], stats[:, 0:1],
                                                  -1.0 / D, None, ALU.mult),
                   waits=[("cv", mq)], incs=[("cv", 1)], ms=True)
            mq = E.emit("vector",
                   lambda e: e.tensor_scalar(stats[:, 3:4], stats[:, 1:2],
                                             1.0 / D, EPS, ALU.mult, ALU.add),
                   waits=[("cv", mq)], incs=[("cv", 1)], ms=True)
            mq = E.emit("vector",
                   lambda e, o=o: e.tensor_scalar(stats[:, 4:5], stats[:, o:o + 1],
                                                  stats[:, o:o + 1], None, ALU.mult),
                   waits=[("cv", mq)], incs=[("cv", 1)], ms=True)
            mq = E.emit("vector",
                   lambda e: e.tensor_sub(stats[:, 5:6], stats[:, 3:4], stats[:, 4:5]),
                   waits=[("cv", mq)], incs=[("cv", 1)], ms=True)
            ms_var[m] = E.emit("vector",
                         lambda e, o=o: e.tensor_scalar(stats[:, o + 2:o + 3],
                                                        stats[:, 5:6],
                                                        1e-30, None, ALU.max),
                         waits=[("cv", mq)], incs=[("cv", 1)], ms=True)

        def attn_s1(m):
            o = 16 + m * 4
            ms_sqrt[m] = E.emit("scalar",
                         lambda e, o=o: e.activation(stats[:, o + 3:o + 4],
                                                     stats[:, o + 2:o + 3], AF.Sqrt),
                         waits=[("cv", ms_var[m])], incs=[("cs", 1)], ms=True)

        def attn_v2(m):
            o = 16 + m * 4
            ub = ubf[:, (m % 2) * D:(m % 2) * D + D]
            h1b = h1nb[:, (m % 2) * D:(m % 2) * D + D]
            msr = E.emit("vector",
                   lambda e, o=o: e.reciprocal(stats[:, o + 1:o + 2],
                                               stats[:, o + 3:o + 4]),
                   waits=[("cs", ms_sqrt[m])], incs=[("cv", 1)], ms=True)
            wh = ([(f"sHN{m % 2}", ms_h1free[m - 2])] if m >= 2 else []) + [("cv", msr)]
            ms_norm[m] = E.emit("vector",
                                lambda e, o=o, h1b=h1b, ub=ub: e.tensor_scalar(
                                    h1b, ub, stats[:, o:o + 1], stats[:, o + 1:o + 2],
                                    ALU.add, ALU.mult),
                                waits=wh, incs=[("cv", 1)], ms=True)

        def attn_s2(m):
            h1b = h1nb[:, (m % 2) * D:(m % 2) * D + D]
            w_tp = [("cv", ms_norm[m])] + \
                   ([(f"sHS{(m - 1) % 2}", ms_h1tp[m - 1])] if m >= 1 else [])
            mtp = E.emit("sync",
                   lambda e, m=m, h1b=h1b: e.dma_start_transpose(
                       tbuf[:].rearrange("p (t q) -> p t q", t=KB),
                       h1b),
                   waits=w_tp, incs=[(f"sHT{m % 2}", 16)], ms=True)
            ms_h1tp[m] = E.emit("scalar",
                   lambda e, m=m: e.dma_start(
                       out=h1t_d[:, m * 128:(m + 1) * 128]
                           .rearrange("(k p) q -> p k q", p=128),
                       in_=tbuf[:].rearrange("p (t q) -> p t q", t=KB)),
                   waits=[(f"sHT{m % 2}", mtp)], incs=[(f"sHS{m % 2}", 16)], ms=True)
            ms_h1free[m] = E.emit("scalar",
                                  lambda e, m=m, h1b=h1b: e.dma_start(
                                      out=h1n_d[m * 128:(m + 1) * 128, :], in_=h1b),
                                  incs=[(f"sHN{m % 2}", 16)], ms=True)

        att_v1_next = [0]
        att_n_next = [0]
        att_fin_next = [0]

        def attn_tick():
            if att_n_next[0] < att_v1_next[0] and att_n_next[0] < MT:
                m = att_n_next[0]
                if m >= 2:
                    attn_s2(m - 2)
                    att_fin_next[0] = m - 1
                attn_s1(m)
                attn_v2(m)
                att_n_next[0] += 1
            elif att_n_next[0] >= MT and att_fin_next[0] < MT:
                attn_s2(att_fin_next[0])
                att_fin_next[0] += 1
            if att_v1_next[0] < MT:
                m = att_v1_next[0]
                attn_v1(m)
                emit_x_dma(m + 1)
                att_v1_next[0] += 1

        ms_w1 = {}
        ms_w2 = {}
        ms_p1 = {}
        ms_p2 = {}
        jobs = []
        w1_ji = {}
        w2_ji = {}
        w3_last_ji = {}
        w3list = [(cc, b) for cc in range(3) for b in range(KB)]
        w3ptr = 0
        for k in range(HBK):
            w1_ji[k] = len(jobs)
            jobs.append((0, w1f_d[k * 128:(k + 1) * 128, :], D,
                         q1t_d[k * 128:(k + 1) * 128, :], False))
            w2_ji[k] = len(jobs)
            jobs.append((1, w2f_d[k * 128:(k + 1) * 128, :], D,
                         q2t_d[k * 128:(k + 1) * 128, :], False))
            if w3ptr < len(w3list):
                cc, b = w3list[w3ptr]
                Fc = 1536 if cc == 2 else 2048
                w3_last_ji[cc] = len(jobs)
                jobs.append((2, w3f_d[b * 128:(b + 1) * 128, cc * 2048:cc * 2048 + Fc],
                             Fc,
                             q3t_d[cc * 2048:cc * 2048 + Fc, b * 128:(b + 1) * 128]
                             .rearrange("(t p) q -> p t q", p=128), True))
                w3ptr += 1
        while w3ptr < len(w3list):
            cc, b = w3list[w3ptr]
            Fc = 1536 if cc == 2 else 2048
            w3_last_ji[cc] = len(jobs)
            jobs.append((2, w3f_d[b * 128:(b + 1) * 128, cc * 2048:cc * 2048 + Fc],
                         Fc,
                         q3t_d[cc * 2048:cc * 2048 + Fc, b * 128:(b + 1) * 128]
                         .rearrange("(t p) q -> p t q", p=128), True))
            w3ptr += 1
        NJ = len(jobs)
        job_hb = {}
        for k in range(HBK):
            job_hb[w1_ji[k]] = k
            job_hb[w2_ji[k]] = k
        ms_in = [None] * NJ
        ms_s1 = [None] * NJ
        ms_v2 = [None] * NJ
        ms_cmb = [None] * NJ
        ms_tp = [None] * NJ
        ms_st = [None] * NJ

        def q_in(ji):
            if ji >= NJ:
                return
            wi, src, F, dst, is3 = jobs[ji]
            pin_s = pin[:, (ji % 3) * D:(ji % 3) * D + F]
            w_in = []
            if ji >= 3:
                w_in += [("cs", ms_s1[ji - 3]), ("cv", ms_v2[ji - 3])]
            ms_in[ji] = E.emit("scalar" if ji % 2 == 0 else "gpsimd",
                               lambda e, pin_s=pin_s, src=src: e.dma_start(
                                   out=pin_s, in_=src),
                               waits=w_in, incs=[(f"sP{ji % 3}", 16)], ms=True)

        def q_sgn(ji):
            wi, src, F, dst, is3 = jobs[ji]
            pin_s = pin[:, (ji % 3) * D:(ji % 3) * D + F]
            s1_s = s1b[:, (ji % 2) * D:(ji % 2) * D + F]
            w_s = [(f"sP{ji % 3}", ms_in[ji]), ("cv", ms_th[wi])] + \
                  ([("cv", ms_cmb[ji - 2])] if ji >= 2 else [])
            ms_s1[ji] = E.emit("scalar",
                               lambda e, s1_s=s1_s, pin_s=pin_s, wi=wi: e.sign(
                                   s1_s, pin_s, bias=thc[:, 4 + wi:5 + wi]),
                               waits=w_s, incs=[("cs", 1)], ms=True)

        def q_lt(ji):
            wi, src, F, dst, is3 = jobs[ji]
            pin_s = pin[:, (ji % 3) * D:(ji % 3) * D + F]
            v2_s = v2b[:, (ji % 2) * D:(ji % 2) * D + F]
            w_v = [(f"sP{ji % 3}", ms_in[ji]), ("cv", ms_th[wi])] + \
                  ([("cv", ms_cmb[ji - 2])] if ji >= 2 else [])
            ms_v2[ji] = E.emit("vector",
                               lambda e, v2_s=v2_s, pin_s=pin_s, wi=wi: e.tensor_scalar(
                                   v2_s, pin_s, thc[:, 4 + wi:5 + wi], 0.5,
                                   ALU.is_lt, ALU.subtract),
                               waits=w_v, incs=[("cv", 1)], ms=True)

        def q_cmb(ji):
            if ji < 0 or ji >= NJ:
                return
            wi, src, F, dst, is3 = jobs[ji]
            s1_s = s1b[:, (ji % 2) * D:(ji % 2) * D + F]
            v2_s = v2b[:, (ji % 2) * D:(ji % 2) * D + F]
            qn_s = qnb[:, (ji % 2) * D:(ji % 2) * D + F]
            w_c = [("cs", ms_s1[ji])]
            if ji >= 2:
                tsem, tval = ms_tp[ji - 2]
                w_c.append((tsem, tval))
            ms_cmb[ji] = E.emit("vector",
                                lambda e, qn_s=qn_s, v2_s=v2_s, s1_s=s1_s:
                                e.scalar_tensor_tensor(qn_s, v2_s, -2.0, s1_s,
                                                       ALU.mult, ALU.add),
                                waits=w_c, incs=[("cv", 1)], ms=True)

        def q_tp(ji):
            if ji < 0 or ji >= NJ:
                return
            wi, src, F, dst, is3 = jobs[ji]
            qn_s = qnb[:, (ji % 2) * D:(ji % 2) * D + F]
            nt = F // 128
            if wi == 2:
                qt_s = qtb[:, (ji % 2) * D:(ji % 2) * D + F]
                w_t = [("cv", ms_cmb[ji])]
                if ji >= 2:
                    w_t.append((f"sS{ji % 2}", ms_st[ji - 2]))
                ms_tp[ji] = (f"sT{ji % 2}", E.emit(
                    "sync",
                    lambda e, qt_s=qt_s, qn_s=qn_s, nt=nt: e.dma_start_transpose(
                        qt_s.rearrange("p (t q) -> p t q", t=nt), qn_s),
                    waits=w_t, incs=[(f"sT{ji % 2}", 16)], ms=True))
            else:
                hb = job_hb[ji]
                wt = w1t if wi == 0 else w2t
                sem = f"sW1{hb % 2}" if wi == 0 else f"sW2{hb % 2}"
                dstt = wt[:, (hb % 2) * D:(hb % 2) * D + D]
                w_t = [("cv", ms_cmb[ji])]
                if hb >= 2:
                    pms = ms_p1.get((0, hb - 2)) if wi == 0 else ms_p2.get((0, hb - 2))
                    assert pms is not None, (ji, hb)
                    w_t.append(("ct", pms))
                    prev_ji = w1_ji[hb - 2] if wi == 0 else w2_ji[hb - 2]
                    w_t.append((f"sS{prev_ji % 2}", ms_st[prev_ji]))
                val = E.emit(
                    "sync",
                    lambda e, dstt=dstt, qn_s=qn_s, nt=nt: e.dma_start_transpose(
                        dstt.rearrange("p (t q) -> p t q", t=nt), qn_s),
                    waits=w_t, incs=[(sem, 16)], ms=True)
                ms_tp[ji] = (sem, val)
                if wi == 0:
                    ms_w1[(0, hb)] = val
                else:
                    ms_w2[(0, hb)] = val

        def q_st(ji):
            if ji < 0 or ji >= NJ:
                return
            wi, src, F, dst, is3 = jobs[ji]
            nt = F // 128
            if wi == 2:
                qt_s = qtb[:, (ji % 2) * D:(ji % 2) * D + F]
                src_st = qt_s.rearrange("p (t q) -> p t q", t=nt)
            else:
                hb = job_hb[ji]
                wt = w1t if wi == 0 else w2t
                src_st = wt[:, (hb % 2) * D:(hb % 2) * D + D]
            tsem, tval = ms_tp[ji]
            ms_st[ji] = E.emit("gpsimd",
                               lambda e, dst=dst, src_st=src_st: e.dma_start(
                                   out=dst, in_=src_st),
                               waits=[(tsem, tval)],
                               incs=[(f"sS{ji % 2}", 16)], ms=True)

        job_next = [0]

        def q_tick():
            ji = job_next[0]
            if ji >= NJ + 2:
                return False
            if ji == 0:
                q_in(0)
                q_in(1)
            q_in(ji + 2)
            q_st(ji - 2)
            if ji < NJ:
                q_sgn(ji)
                q_lt(ji)
            q_cmb(ji - 1)
            q_tp(ji - 1)
            job_next[0] += 1
            return True

        def prep_done_waits():
            w = [("cs", ms_s1[NJ - 1]), ("cv", ms_cmb[NJ - 1])]
            w += [(f"sP{i}", E.n[f"sP{i}"]) for i in range(3)]
            w += [(f"sT{i}", E.n[f"sT{i}"]) for i in range(2)]
            w += [(f"sS{i}", E.n[f"sS{i}"]) for i in range(2)]
            return w

        emit_x_dma(0)
        for _ in range(8):
            attn_tick()
        for _ in range(6):
            q_tick()

        def attn_done_waits():
            return ([(f"sHT{i}", E.n[f"sHT{i}"]) for i in range(2)] +
                    [(f"sHS{i}", E.n[f"sHS{i}"]) for i in range(2)] +
                    [(f"sHN{i}", E.n[f"sHN{i}"]) for i in range(2)] +
                    [(f"sX{i}", E.n[f"sX{i}"]) for i in range(2)] +
                    [(f"sG{i}", E.n[f"sG{i}"]) for i in range(2)] +
                    [("cg", ms_xfree[MT - 1]), ("cv", ms_norm[MT - 1]),
                     ("cs", ms_gab[NCHUNK - 1])])
        attn_st.close()
        post_st = contextlib.ExitStack()
        y2acc = post_st.enter_context(nc.sbuf_tensor("y2acc", [128, 4 * D], F32))
        w3t = post_st.enter_context(nc.sbuf_tensor("w3t", [128, 2 * G3 * 512], BF16))
        first_post = [2]
        first_evac = [True]

        ms_h1q = [None]
        ms_sig = {}
        ms_hT = {}
        ms_w3l = {}
        ms_f3u = {}
        ms_ev = {}
        ms_f3last = {}
        ln2_first = [True]
        ms_ln2free = [None] * 4
        ms_outd = [None] * 4
        ln2_bufs = {}

        def emit_h1q(q):
            w = [("cv", ms_norm[4 * q + 3]),
                 (f"sHS{(4 * q + 3) % 2}", ms_h1tp[4 * q + 3]),
                 (f"sHS{(4 * q + 2) % 2}", ms_h1tp[4 * q + 2])]
            if q > 0:
                w.append(("ct", ms_p2[(q - 1, HBK - 1)]))
            ms_h1q[0] = E.emit(
                "sync",
                lambda e, q=q: e.dma_start(
                    out=h1Th[:].rearrange("p (k n) -> p k n", k=KB),
                    in_=h1t_d[:, q * QT:(q + 1) * QT]
                        .rearrange("(k p) n -> p k n", p=128)),
                waits=w, incs=[("sH1", 16)], ms=True)

        def emit_w_load(q, hb):
            if hb >= HBK:
                return
            j1, j2 = w1_ji[hb], w2_ji[hb]
            assert ms_st[j1] is not None and ms_st[j2] is not None, (hb, j1, j2)
            w_1 = [(f"sS{j1 % 2}", ms_st[j1])]
            w_2x = []
            if q == 1:
                ps1 = w1_ji[HBK - 2 + hb] if hb < 2 else None
                ps2 = w2_ji[HBK - 2 + hb] if hb < 2 else None
                if ps1 is not None:
                    w_1.append((f"sS{ps1 % 2}", ms_st[ps1]))
                    w_2x.append((f"sS{ps2 % 2}", ms_st[ps2]))
            prev = (q, hb - 2) if hb >= 2 else ((q - 1, HBK - 2 + hb) if q > 0 else None)
            if prev in ms_p1:
                w_1.append(("ct", ms_p1[prev]))
            ms_w1[(q, hb)] = E.emit(
                "sync",
                lambda e, hb=hb: e.dma_start(
                    out=w1t[:, (hb % 2) * D:(hb % 2) * D + D],
                    in_=q1t_d[hb * 128:(hb + 1) * 128, :]),
                waits=w_1, incs=[(f"sW1{hb % 2}", 16)], ms=True)
            w_2 = [(f"sS{j2 % 2}", ms_st[j2])] + w_2x
            if prev in ms_p2:
                w_2.append(("ct", ms_p2[prev]))
            ms_w2[(q, hb)] = E.emit(
                "sync",
                lambda e, hb=hb: e.dma_start(
                    out=w2t[:, (hb % 2) * D:(hb % 2) * D + D],
                    in_=q2t_d[hb * 128:(hb + 1) * 128, :]),
                waits=w_2, incs=[(f"sW2{hb % 2}", 16)], ms=True)

        def emit_f12(q, hb):
            wslab1 = w1t[:, (hb % 2) * D:(hb % 2) * D + D]
            wslab2 = w2t[:, (hb % 2) * D:(hb % 2) * D + D]
            ps1 = px1[:, (hb % 2) * QT:(hb % 2) * QT + QT]
            ps2 = px2[:, (hb % 2) * QT:(hb % 2) * QT + QT]
            w_mm = [(f"sW1{hb % 2}", ms_w1[(q, hb)]), ("sH1", ms_h1q[0])]
            prevsig = (q, hb - 2) if hb >= 2 else ((q - 1, HBK - 2 + hb) if q > 0 else None)
            if prevsig in ms_sig:
                w_mm.append(("cs", ms_sig[prevsig]))
            for k in range(KB):
                def mk(e, k=k, wslab=wslab1, ps1=ps1):
                    return e.matmul(ps1, wslab[:, k * 128:(k + 1) * 128],
                                    h1Th[:, k * QT:(k + 1) * QT],
                                    start=(k == 0), stop=(k == KB - 1))
                if k == 0:
                    E.emit("tensor", mk, waits=w_mm)
                elif k == KB - 1:
                    ms_p1[(q, hb)] = E.emit("tensor", mk, incs=[("ct", 1)], ms=True)
                else:
                    E.emit("tensor", mk)
            w_mm2 = [(f"sW2{hb % 2}", ms_w2[(q, hb)])]
            prevhT = (q, hb - 2) if hb >= 2 else ((q - 1, HBK - 2 + hb) if q > 0 else None)
            if prevhT in ms_hT:
                w_mm2.append(("cv", ms_hT[prevhT]))
            for k in range(KB):
                def mk2(e, k=k, wslab=wslab2, ps2=ps2):
                    return e.matmul(ps2, wslab[:, k * 128:(k + 1) * 128],
                                    h1Th[:, k * QT:(k + 1) * QT],
                                    start=(k == 0), stop=(k == KB - 1))
                if k == 0:
                    E.emit("tensor", mk2, waits=w_mm2)
                elif k == KB - 1:
                    ms_p2[(q, hb)] = E.emit("tensor", mk2, incs=[("ct", 1)], ms=True)
                else:
                    E.emit("tensor", mk2)
            w_sig = [("ct", ms_p1[(q, hb)])]
            if prevhT in ms_hT:
                w_sig.append(("cv", ms_hT[prevhT]))
            ms_sig[(q, hb)] = E.emit(
                "scalar",
                lambda e, hb=hb, ps1=ps1: e.activation(
                    sgb[:, (hb % 2) * QT:(hb % 2) * QT + QT],
                    ps1, AF.Sigmoid, scale=0.5),
                waits=w_sig, incs=[("cs", 1)], ms=True)
            w_ht = [("cs", ms_sig[(q, hb)]), ("ct", ms_p2[(q, hb)])]
            prevf3 = (q - 1, hb // G3)
            if prevf3 in ms_f3last:
                w_ht.append(("ct", ms_f3last[prevf3]))
            ms_hT[(q, hb)] = E.emit(
                "vector",
                lambda e, hb=hb, ps2=ps2: e.tensor_mul(
                    hTq[:, hb * QT:(hb + 1) * QT],
                    sgb[:, (hb % 2) * QT:(hb % 2) * QT + QT], ps2),
                waits=w_ht, incs=[("cv", 1)], ms=True)

        def emit_w3_load(q, g, dq, prev):
            w = []
            if first_post[0] > 0:
                first_post[0] -= 1
                w += attn_done_waits()
            if prev in ms_f3u:
                w.append(("ct", ms_f3u[prev]))
            lo_c = (g * G3 * 128) // 2048
            hi_c = ((g + 1) * G3 * 128 - 1) // 2048
            for ccx in range(lo_c, hi_c + 1):
                jid = w3_last_ji[ccx]
                assert ms_st[jid] is not None, (g, dq, ccx, jid)
                w.append((f"sS{jid % 2}", ms_st[jid]))
            ms_w3l[(q, g, dq)] = E.emit(
                "sync",
                lambda e, g=g, dq=dq: e.dma_start(
                    out=w3t[:, (dq % 2) * G3 * 512:(dq % 2) * G3 * 512 + G3 * 512]
                        .rearrange("p (t n) -> p t n", t=G3),
                    in_=q3t_d[g * G3 * 128:(g + 1) * G3 * 128, dq * 512:(dq + 1) * 512]
                        .rearrange("(t p) n -> p t n", p=128)),
                waits=w, incs=[(f"sW3{dq % 2}", 16)], ms=True)

        def emit_f3(q, g):
            unit = 0
            msu = None
            for dq in range(4):
                if dq == 2:
                    emit_w3_load(q, g, 2, (q, g, 0))
                    emit_w3_load(q, g, 3, (q, g, 1))
                for m in range(4):
                    pp = pyA if unit % 2 == 0 else pyB
                    w_u = [("cv", ms_hT[(q, (g + 1) * G3 - 1)]),
                           (f"sW3{dq % 2}", ms_w3l[(q, g, dq)])]
                    prev_ev = None
                    if unit >= 2:
                        prev_ev = ms_ev.get((q, g, unit - 2))
                    elif g >= 1:
                        prev_ev = ms_ev.get((q, g - 1, 14 + unit))
                    elif q > 0:
                        prev_ev = ms_ev.get((q - 1, NG3 - 1, 14 + unit))
                    if prev_ev is not None:
                        w_u.append(("cv", prev_ev))
                    for kk in range(G3):
                        def mk3(e, kk=kk, m=m, dq=dq, g=g, pp=pp):
                            return e.matmul(
                                pp[:],
                                hTq[:, (g * G3 + kk) * QT + m * 128:
                                       (g * G3 + kk) * QT + (m + 1) * 128],
                                w3t[:, (dq % 2) * G3 * 512 + kk * 512:
                                       (dq % 2) * G3 * 512 + kk * 512 + 512],
                                start=(kk == 0), stop=(kk == G3 - 1))
                        if kk == 0:
                            E.emit("tensor", mk3, waits=w_u)
                        elif kk == G3 - 1:
                            msu = E.emit("tensor", mk3, incs=[("ct", 1)], ms=True)
                        else:
                            E.emit("tensor", mk3)
                    ms_f3u[(q, g, dq)] = msu
                    ydst = y2acc[:, m * D + dq * 512: m * D + dq * 512 + 512]
                    w_e = [("ct", msu)]
                    if first_evac[0]:
                        first_evac[0] = False
                        w_e += attn_done_waits()
                    if g == 0:
                        mse = E.emit("vector",
                                     lambda e, ydst=ydst, pp=pp: e.tensor_copy(ydst, pp[:]),
                                     waits=w_e, incs=[("cv", 1)], ms=True)
                    else:
                        mse = E.emit("vector",
                                     lambda e, ydst=ydst, pp=pp: e.scalar_tensor_tensor(
                                         ydst, pp[:], 1.0, ydst, ALU.bypass, ALU.add),
                                     waits=w_e, incs=[("cv", 1)], ms=True)
                    ms_ev[(q, g, unit)] = mse
                    unit += 1
            if g + 1 < NG3:
                emit_w3_load(q, g + 1, 0, (q, g, 2))
                emit_w3_load(q, g + 1, 1, (q, g, 3))
            elif q + 1 < NQ:
                emit_w3_load(q + 1, 0, 0, (q, g, 2))
                emit_w3_load(q + 1, 0, 1, (q, g, 3))
            ms_f3last[(q, g)] = msu

        def alloc_ln2():
            ln2_bufs["uU"] = pin[:, 0:D]
            ln2_bufs["outb"] = pin[:, D:3 * D]
            ln2_bufs["h1nb2"] = s1b[:, 0:2 * D]

        def emit_ln2_m(q, m):
            uU_a = ln2_bufs["uU"]
            h1nb2_a = ln2_bufs["h1nb2"]
            outb_a = ln2_bufs["outb"]
            last_ev = ms_ev[(q, NG3 - 1, 15)]
            if True:
                mg = q * 4 + m
                w_hn = [("cv", ms_ln2free[(m - 2) % 4])] if m >= 2 else []
                if ln2_first[0]:
                    ln2_first[0] = False
                    w_hn = prep_done_waits() + w_hn
                elif m < 2 and q > 0:
                    w_hn = [("cv", ms_ln2free[(m + 2) % 4])] + w_hn
                ms_hnm = E.emit(
                    "sync",
                    lambda e, mg=mg, m=m: e.dma_start(
                        out=h1nb2_a[:, (m % 2) * D:(m % 2) * D + D],
                        in_=h1n_d[mg * 128:(mg + 1) * 128, :]),
                    waits=w_hn, incs=[(f"sHB{m % 2}", 16)], ms=True)
                h1s = h1nb2_a[:, (m % 2) * D:(m % 2) * D + D]
                yv = y2acc[:, m * D:(m + 1) * D]
                ob = outb_a[:, (m % 2) * D:(m % 2) * D + D]
                E.emit("vector",
                       lambda e, h1s=h1s, yv=yv: e.scalar_tensor_tensor(
                           uU_a, h1s, 4.0, yv, ALU.mult, ALU.add,
                           accum_out=st2[:, 0:1]),
                       waits=[(f"sHB{m % 2}", ms_hnm), ("cv", last_ev)],
                       incs=[("cv", 1)])
                w_sq = [(f"sO{m % 2}", ms_outd[(m - 2) % 4])] if m >= 2 else \
                       ([(f"sO{m % 2}", ms_outd[(m + 2) % 4])] if q > 0 else [])
                mq2 = E.emit("vector",
                       lambda e, ob=ob: e.scalar_tensor_tensor(
                           ob, uU_a, 1.0, uU_a, ALU.bypass, ALU.mult,
                           accum_out=st2[:, 1:2]),
                       waits=w_sq, incs=[("cv", 1)], ms=True)
                mq2 = E.emit("vector",
                       lambda e: e.tensor_scalar(st2[:, 2:3], st2[:, 0:1],
                                                 -1.0 / D, None, ALU.mult),
                       waits=[("cv", mq2)], incs=[("cv", 1)], ms=True)
                mq2 = E.emit("vector",
                       lambda e: e.tensor_scalar(st2[:, 3:4], st2[:, 1:2],
                                                 1.0 / D, EPS16, ALU.mult, ALU.add),
                       waits=[("cv", mq2)], incs=[("cv", 1)], ms=True)
                mq2 = E.emit("vector",
                       lambda e: e.tensor_scalar(st2[:, 4:5], st2[:, 2:3],
                                                 st2[:, 2:3], None, ALU.mult),
                       waits=[("cv", mq2)], incs=[("cv", 1)], ms=True)
                mq2 = E.emit("vector",
                       lambda e: e.tensor_sub(st2[:, 5:6], st2[:, 3:4], st2[:, 4:5]),
                       waits=[("cv", mq2)], incs=[("cv", 1)], ms=True)
                msv2 = E.emit("vector",
                              lambda e: e.tensor_scalar(st2[:, 5:6], st2[:, 5:6],
                                                        1e-30, None, ALU.max),
                              waits=[("cv", mq2)], incs=[("cv", 1)], ms=True)
                mss2 = E.emit("scalar",
                              lambda e: e.activation(st2[:, 6:7], st2[:, 5:6], AF.Sqrt),
                              waits=[("cv", msv2)], incs=[("cs", 1)], ms=True)
                msr2 = E.emit("vector", lambda e: e.reciprocal(st2[:, 7:8], st2[:, 6:7]),
                       waits=[("cs", mss2)], incs=[("cv", 1)], ms=True)
                ms_ln2free[m % 4] = E.emit(
                    "vector",
                    lambda e, ob=ob: e.tensor_scalar(ob, uU_a, st2[:, 2:3],
                                                     st2[:, 7:8],
                                                     ALU.add, ALU.mult),
                    waits=[("cv", msr2)], incs=[("cv", 1)], ms=True)
                ms_outd[m % 4] = E.emit(
                    "sync",
                    lambda e, mg=mg, m=m: e.dma_start(
                        out=out_d[mg * 128:(mg + 1) * 128, :],
                        in_=outb_a[:, (m % 2) * D:(m % 2) * D + D]),
                    waits=[("cv", ms_ln2free[m % 4])], incs=[(f"sO{m % 2}", 16)], ms=True)
                if _DEBUG_Y2:
                    E.emit("sync",
                           lambda e, mg=mg, m=m: e.dma_start(
                               out=dbgy2_d[mg * 128:(mg + 1) * 128, :],
                               in_=y2acc[:, m * D:(m + 1) * D]),
                           waits=[("cv", ms_ev[(q, NG3 - 1, 15)])],
                           incs=[("sDBG", 16)], ms=True)

        with post_st:
            alloc_ln2()
            for q in range(NQ):
                if q > 0:
                    while att_fin_next[0] < min(4 * (q + 1), MT):
                        attn_tick()
                        if att_fin_next[0] >= MT:
                            break
                emit_h1q(q)
                if q > 0:
                    emit_w_load(q, 0)
                    emit_w_load(q, 1)
                for hb in range(HBK):
                    if q == 0:
                        assert (0, hb) in ms_w1 and (0, hb) in ms_w2, hb
                    emit_f12(q, hb)
                    if q == 0:
                        q_tick()
                        q_tick()
                        q_tick()
                        if hb % 4 == 3:
                            attn_tick()
                    else:
                        emit_w_load(q, hb + 2)
                    if q > 0 and hb in (4, 14, 24, 34):
                        emit_ln2_m(q - 1, (hb - 4) // 10)
                if q == 0:
                    while job_next[0] < NJ + 2:
                        q_tick()
                    while att_fin_next[0] < MT:
                        attn_tick()
                if q == 0:
                    emit_w3_load(0, 0, 0, None)
                    emit_w3_load(0, 0, 1, None)
                for g in range(NG3):
                    emit_f3(q, g)
            emit_ln2_m(NQ - 1, 0)
            emit_ln2_m(NQ - 1, 1)
            emit_ln2_m(NQ - 1, 2)
            emit_ln2_m(NQ - 1, 3)

            all_dma = [s for s in sem_names if s.startswith("s")]
            E.emit("sync", lambda e: e.nop(),
                   waits=[(nm, E.n[nm]) for nm in all_dma if E.n[nm] > 0])
            E.run_block(nc)
        prep_st.close()
    return nc


def kernel(L0_attn_x, L0_attn_weight, L0_attn_bias, L0_attn_decay_value,
           L0_ffn_w1, L0_ffn_w2, L0_ffn_w3):
    x = np.ascontiguousarray(np.asarray(L0_attn_x, dtype=np.float32)
                             .reshape(NCORE * TOK, D))
    wgt = np.asarray(L0_attn_weight, dtype=np.float32)
    bias = np.asarray(L0_attn_bias, dtype=np.float32)
    dv = np.asarray(L0_attn_decay_value, dtype=np.float32)
    w1 = np.ascontiguousarray(np.asarray(L0_ffn_w1, dtype=np.float32))
    w2 = np.ascontiguousarray(np.asarray(L0_ffn_w2, dtype=np.float32))
    w3 = np.ascontiguousarray(np.asarray(L0_ffn_w3, dtype=np.float32))

    w0b = np.ascontiguousarray(np.broadcast_to(wgt[0:1, :], (128, D)))
    w1b = np.ascontiguousarray(np.broadcast_to(wgt[1:2, :], (128, D)))
    biasb = np.ascontiguousarray(np.broadcast_to(bias.reshape(1, D), (128, D)))
    dvrep = np.ascontiguousarray(np.broadcast_to(dv.reshape(1, 2), (128, 2)))
    onesrow = np.ones((1, 128), np.float32)
    onescol = np.ones((128, 1), np.float32)
    epsh = np.full((1, 1), 0.5e-8, np.float32)
    wflat = [w1.reshape(-1), w2.reshape(-1), w3.reshape(-1)]

    if "prog" not in _NC_CACHE:
        _NC_CACHE["prog"] = build_program()
    nc = _NC_CACHE["prog"]

    in_maps = []
    for c in range(NCORE):
        wsl = np.stack([wf[c * SLICE:(c + 1) * SLICE] for wf in wflat])
        in_maps.append({
            "x": x[c * TOK:(c + 1) * TOK],
            "w0b": w0b, "w1b": w1b, "biasb": biasb, "dvrep": dvrep,
            "onesrow": onesrow, "onescol": onescol, "epsh": epsh,
            "wsl": np.ascontiguousarray(wsl),
            "w1f": w1, "w2f": w2, "w3f": w3,
        })
    res = run_bass_kernel_spmd(nc, in_maps, core_ids=list(range(NCORE)))
    out = np.concatenate([np.asarray(r["out"]) for r in res.results], axis=0)
    return out.reshape(2, 8192, D)


if __name__ == "__main__":
    print("building program ...")
    build_program()
    print("built ok")


# version 17
# speedup vs baseline: 1.1088x; 1.1088x over previous
import contextlib
import numpy as np
import concourse.bass as bass
import concourse.mybir as mybir
from concourse.bass_utils import run_bass_kernel_spmd

F32 = mybir.dt.float32
BF16 = mybir.dt.bfloat16
AF = mybir.ActivationFunctionType
ALU = mybir.AluOpType

NCORE = 8
TOK = 2048
D = 2048
KB = 16
H = 5632
HBK = 44
MT = 16
QT = 512
NQ = 4
G3 = 11
NG3 = HBK // G3
EPS = 1e-5
EPS16 = 16.0 * EPS
WELEM = float(H * D)
SLICE = H * D // NCORE
GCH = 1408
NGC = SLICE // (128 * GCH)

_NC_CACHE = {}
_DEBUG_Y2 = False


class Emitter:
    def __init__(self, sems, counts=None):
        self.q = {k: [] for k in ("sync", "scalar", "vector", "tensor", "gpsimd")}
        self.n = dict(counts) if counts else {name: 0 for name in sems}
        self.sems = sems

    def emit(self, eng, fn, waits=(), incs=(), ms=False):
        waits = [(s, v) for (s, v) in waits if v is not None and v > 0]
        incs = list(incs)
        after = None
        for s, k in incs:
            self.n[s] += k
            after = self.n[s]
        self.q[eng].append((fn, waits, incs))
        return after if ms else None

    def replay(self, eng_name, eng):
        for fn, waits, incs in self.q[eng_name]:
            for s, v in waits:
                eng.wait_ge(self.sems[s], v)
            ins = fn(eng)
            for s, k in incs:
                ins.then_inc(self.sems[s], k)

    def run_block(self, nc):
        with nc.Block() as block:
            @block.sync
            def _(e):
                self.replay("sync", e)

            @block.scalar
            def _(e):
                self.replay("scalar", e)

            @block.vector
            def _(e):
                self.replay("vector", e)

            @block.tensor
            def _(e):
                self.replay("tensor", e)

            @block.gpsimd
            def _(e):
                self.replay("gpsimd", e)


def build_program():
    nc = bass.Bass("TRN2", target_bir_lowering=False, debug=False)

    x_d = nc.dram_tensor("x", [TOK, D], F32, kind="ExternalInput").ap()
    w0b_d = nc.dram_tensor("w0b", [128, D], F32, kind="ExternalInput").ap()
    w1b_d = nc.dram_tensor("w1b", [128, D], F32, kind="ExternalInput").ap()
    biasb_d = nc.dram_tensor("biasb", [128, D], F32, kind="ExternalInput").ap()
    dvrep_d = nc.dram_tensor("dvrep", [128, 2], F32, kind="ExternalInput").ap()
    onesrow_d = nc.dram_tensor("onesrow", [1, 128], F32, kind="ExternalInput").ap()
    onescol_d = nc.dram_tensor("onescol", [128, 1], F32, kind="ExternalInput").ap()
    epsh_d = nc.dram_tensor("epsh", [1, 1], F32, kind="ExternalInput").ap()
    wsl_d = nc.dram_tensor("wsl", [3, SLICE], F32, kind="ExternalInput").ap()
    w1f_d = nc.dram_tensor("w1f", [H, D], F32, kind="ExternalInput").ap()
    w2f_d = nc.dram_tensor("w2f", [H, D], F32, kind="ExternalInput").ap()
    w3f_d = nc.dram_tensor("w3f", [D, H], F32, kind="ExternalInput").ap()
    out_d = nc.dram_tensor("out", [TOK, D], F32, kind="ExternalOutput").ap()
    q1t_d = nc.dram_tensor("q1t", [H, D], BF16, kind="Internal").ap()
    q2t_d = nc.dram_tensor("q2t", [H, D], BF16, kind="Internal").ap()
    q3t_d = nc.dram_tensor("q3t", [H, D], BF16, kind="Internal").ap()
    h1t_d = nc.dram_tensor("h1t", [KB * 128, TOK], BF16, kind="Internal").ap()
    h1n_d = nc.dram_tensor("h1n", [TOK, D], BF16, kind="Internal").ap()
    ccin_d = nc.dram_tensor("ccin", [128, 4], F32, kind="Internal").ap()
    ccout_d = nc.dram_tensor("ccout", [128, 4], F32, kind="Internal").ap()
    dbgy2_d = nc.dram_tensor("dbgy2", [TOK, D], F32,
                             kind="ExternalOutput" if _DEBUG_Y2 else "Internal").ap()
    dbgth_d = nc.dram_tensor("dbgth", [128, 64], F32,
                             kind="ExternalOutput" if _DEBUG_Y2 else "Internal").ap()

    sem_names = (["cv", "cs", "ct", "cg", "sK", "sDBG", "sH1", "sCC", "sCC2"] +
                 [f"sX{i}" for i in range(2)] + [f"sG{i}" for i in range(2)] +
                 [f"sP{i}" for i in range(3)] + [f"sT{i}" for i in range(2)] +
                 [f"sS{i}" for i in range(2)] + [f"sHT{i}" for i in range(2)] +
                 [f"sHS{i}" for i in range(2)] + [f"sHN{i}" for i in range(2)] +
                 [f"sW1{i}" for i in range(2)] + [f"sW2{i}" for i in range(2)] +
                 [f"sW3{i}" for i in range(2)] +
                 [f"sHB{i}" for i in range(2)] + [f"sO{i}" for i in range(2)])
    stack = contextlib.ExitStack()
    sems = {nm: stack.enter_context(nc.semaphore(nm)) for nm in sem_names}

    with stack:
        thc = stack.enter_context(nc.sbuf_tensor("thc", [128, 8], F32))
        onesr = stack.enter_context(nc.sbuf_tensor("onesr", [1, 128], F32))
        onesc = stack.enter_context(nc.sbuf_tensor("onesc", [128, 1], F32))
        epshs = stack.enter_context(nc.sbuf_tensor("epshs", [1, 4], F32))
        gacc = stack.enter_context(nc.sbuf_tensor("gacc", [128, 4], F32))
        thsml = stack.enter_context(nc.sbuf_tensor("thsml", [1, 4], F32))
        stats = stack.enter_context(nc.sbuf_tensor("stats", [128, 16], F32))
        st2 = stack.enter_context(nc.sbuf_tensor("st2", [128, 8], F32))
        h1Th = stack.enter_context(nc.sbuf_tensor("h1Th", [128, KB * QT], BF16))
        w1t = stack.enter_context(nc.sbuf_tensor("w1t", [128, 2 * D], BF16))
        w2t = stack.enter_context(nc.sbuf_tensor("w2t", [128, 2 * D], BF16))
        sgb = stack.enter_context(nc.sbuf_tensor("sgb", [128, 2 * QT], BF16))
        prep_st = contextlib.ExitStack()
        pin = prep_st.enter_context(nc.sbuf_tensor("pin", [128, 3 * D], F32))
        s1b = prep_st.enter_context(nc.sbuf_tensor("s1b", [128, 2 * D], BF16))
        v2b = prep_st.enter_context(nc.sbuf_tensor("v2b", [128, 2 * D], BF16))
        qnb = prep_st.enter_context(nc.sbuf_tensor("qnb", [128, 2 * D], BF16))
        qtb = prep_st.enter_context(nc.sbuf_tensor("qtb", [128, 2 * D], BF16))
        attn_st = contextlib.ExitStack()
        xin = attn_st.enter_context(nc.sbuf_tensor("xin", [128, 2 * D], F32))
        sA = attn_st.enter_context(nc.sbuf_tensor("sA", [128, D], F32))
        sB = attn_st.enter_context(nc.sbuf_tensor("sB", [128, D], F32))
        sC = attn_st.enter_context(nc.sbuf_tensor("sC", [128, D], F32))
        sD = attn_st.enter_context(nc.sbuf_tensor("sD", [128, D], F32))
        ubf = attn_st.enter_context(nc.sbuf_tensor("ubf", [128, 3 * D], BF16))
        h1nb = attn_st.enter_context(nc.sbuf_tensor("h1nb", [128, 2 * D], BF16))
        tbuf = attn_st.enter_context(nc.sbuf_tensor("tbuf", [128, 2 * D], BF16))
        w0bs = attn_st.enter_context(nc.sbuf_tensor("w0bs", [128, D], BF16))
        w1bs = attn_st.enter_context(nc.sbuf_tensor("w1bs", [128, D], BF16))
        biasbs = attn_st.enter_context(nc.sbuf_tensor("biasbs", [128, D], BF16))
        dvreps = attn_st.enter_context(nc.sbuf_tensor("dvreps", [128, 2], F32))
        dvc = attn_st.enter_context(nc.sbuf_tensor("dvc", [128, 2], F32))
        gb = attn_st.enter_context(nc.sbuf_tensor("gb", [128, 2 * GCH], F32))
        gjunk = attn_st.enter_context(nc.sbuf_tensor("gjunk", [128, GCH], BF16))
        gch = attn_st.enter_context(nc.sbuf_tensor("gch", [128, 32], F32))
        p_th = stack.enter_context(nc.psum_tensor("p_th", [1, 4], F32))
        p_bc = stack.enter_context(nc.psum_tensor("p_bc", [128, 4], F32))
        px1 = stack.enter_context(nc.psum_tensor("px1", [128, 2 * QT], F32))
        px2 = stack.enter_context(nc.psum_tensor("px2", [128, 2 * QT], F32))
        pyA = stack.enter_context(nc.psum_tensor("pyA", [128, 512], F32))
        pyB = stack.enter_context(nc.psum_tensor("pyB", [128, 512], F32))

        E = Emitter(sems)

        d_rows = None
        for dst, src in ((dvreps[:], dvrep_d), (onesr[:], onesrow_d),
                         (onesc[:], onescol_d), (epshs[0:1, 0:1], epsh_d)):
            E.emit("scalar",
                   lambda e, dst=dst, src=src: e.dma_start(out=dst, in_=src),
                   incs=[("sK", 16)])
        for dst, src in ((w0bs[:], w0b_d), (w1bs[:], w1b_d), (biasbs[:], biasb_d)):
            d_rows = E.emit("gpsimd",
                            lambda e, dst=dst, src=src: e.dma_start(out=dst, in_=src),
                            incs=[("sK", 16)], ms=True)
        ms_dvc = E.emit("vector",
               lambda e: e.tensor_scalar(dvc[:], dvreps[:], 0.9, 1.0, ALU.max, ALU.min),
               waits=[("sK", d_rows)], incs=[("cv", 1)], ms=True)
        pstr = dvc[:].ap[0][0]
        dv0_b = bass.AP(dvc, 0, [[pstr, 128], [0, D]])
        dv1_b = bass.AP(dvc, 1, [[pstr, 128], [0, D]])

        flsl = wsl_d.rearrange("a b -> (a b)")
        NCHUNK = 3 * NGC
        ms_gab = [None] * NCHUNK
        for gidx in range(NCHUNK):
            slot = gidx % 2
            w_g = [("cs", ms_gab[gidx - 2])] if gidx >= 2 else []
            msg = E.emit("sync",
                         lambda e, gidx=gidx, slot=slot: e.dma_start(
                             out=gb[:, slot * GCH:slot * GCH + GCH],
                             in_=flsl[gidx * 128 * GCH:(gidx + 1) * 128 * GCH]
                             .rearrange("(p n) -> p n", p=128)),
                         waits=w_g, incs=[(f"sG{slot}", 16)], ms=True)
            ms_gab[gidx] = E.emit("scalar",
                                  lambda e, slot=slot, gidx=gidx: e.activation(
                                      gjunk[:], gb[:, slot * GCH:slot * GCH + GCH],
                                      AF.Abs, accum_out=gch[:, gidx:gidx + 1]),
                                  waits=[(f"sG{slot}", msg)] +
                                        ([("cs", ms_gab[gidx - 2])] if gidx >= 2 else []),
                                  incs=[("cs", 1)], ms=True)
        mgs = None
        for wi in range(3):
            mgs = E.emit("vector",
                         lambda e, wi=wi: e.reduce_sum(
                             gacc[:, wi:wi + 1], gch[:, wi * NGC:(wi + 1) * NGC],
                             axis=mybir.AxisListType.X),
                         waits=[("cs", ms_gab[NCHUNK - 1])] if wi == 0 else [],
                         incs=[("cv", 1)], ms=True)
        ms_ci = E.emit("sync",
                       lambda e: e.dma_start(out=ccin_d, in_=gacc[:]),
                       waits=[("cv", mgs)], incs=[("sCC", 16)], ms=True)
        ms_cc = E.emit("gpsimd",
                       lambda e: e.collective_compute(
                           "AllReduce", ALU.add,
                           [list(range(NCORE))],
                           [ccin_d.opt()], [ccout_d.opt()]),
                       waits=[("sCC", ms_ci)], incs=[("sCC2", 1)], ms=True)
        ms_co = E.emit("sync",
                       lambda e: e.dma_start(out=gacc[:], in_=ccout_d),
                       waits=[("sCC2", ms_cc)], incs=[("sCC", 16)], ms=True)
        ms_thp = None
        for wi in range(3):
            ms_thp = E.emit(
                "tensor",
                lambda e, wi=wi: e.matmul(p_th[0:1, wi:wi + 1], gacc[:, wi:wi + 1],
                                          onesc[:], start=True, stop=True),
                waits=[("sCC", ms_co), ("sK", d_rows)], incs=[("ct", 1)], ms=True)
        ms_thr = E.emit(
            "vector",
            lambda e: e.tensor_scalar(thsml[0:1, 0:3], p_th[0:1, 0:3],
                                      0.5 / WELEM, 0.5e-8, ALU.mult, ALU.add),
            waits=[("ct", ms_thp)], incs=[("cv", 1)], ms=True)
        ms_thb = E.emit(
            "tensor",
            lambda e: e.matmul(p_bc[:, 0:3], onesr[:], thsml[0:1, 0:3],
                               start=True, stop=True),
            waits=[("cv", ms_thr)], incs=[("ct", 1)], ms=True)
        mcp = E.emit("vector",
                     lambda e: e.tensor_copy(thc[:, 0:3], p_bc[:, 0:3]),
                     waits=[("ct", ms_thb)], incs=[("cv", 1)], ms=True)
        ms_thall = E.emit(
            "vector",
            lambda e: e.tensor_scalar(thc[:, 4:7], thc[:, 0:3], -1.0, None, ALU.mult),
            waits=[("cv", mcp)], incs=[("cv", 1)], ms=True)
        ms_th = [ms_thall, ms_thall, ms_thall]
        if _DEBUG_Y2:
            E.emit("sync", lambda e: e.dma_start(out=dbgth_d[:, 0:8], in_=thc[:]),
                   waits=[("cv", ms_thall)], incs=[("sDBG", 16)])
            E.emit("sync", lambda e: e.dma_start(out=dbgth_d[:, 8:40], in_=gch[:]),
                   waits=[("cv", ms_thall)], incs=[("sDBG", 16)])
            E.emit("sync", lambda e: e.dma_start(out=dbgth_d[:, 40:44], in_=gacc[:]),
                   waits=[("cv", ms_thall)], incs=[("sDBG", 16)])

        ms_x = [None] * MT
        ms_xfree = [None] * MT
        ms_norm = [None] * MT
        ms_h1free = [None] * MT
        ms_h1tp = [None] * MT
        ms_var = [None] * MT
        ms_sqrt = [None] * MT
        ms_uc = [None] * MT
        ms_vadd = [None] * MT

        def emit_x_dma(m):
            if m >= MT:
                return
            waits = [("cg", ms_xfree[m - 2])] if m >= 2 else []
            ms_x[m] = E.emit(
                "scalar",
                lambda e, m=m: e.dma_start(
                    out=xin[:, (m % 2) * D:(m % 2) * D + D],
                    in_=x_d[m * 128:(m + 1) * 128, :]),
                waits=waits, incs=[(f"sX{m % 2}", 16)], ms=True)

        def attn_v1(m):
            xm = xin[:, (m % 2) * D:(m % 2) * D + D]
            ub = ubf[:, (m % 3) * D:(m % 3) * D + D]
            waits = [(f"sX{m % 2}", ms_x[m]), ("cv", ms_dvc), ("sK", d_rows)]
            if m >= 1:
                waits.append(("cv", ms_uc[m - 1]))
            mg_u0 = E.emit("gpsimd", lambda e: e.tensor_mul(sA[:], xm, w0bs[:]),
                           waits=waits, incs=[("cg", 1)], ms=True)
            msc_r = E.emit("vector",
                   lambda e: e.tensor_tensor_scan(sB[:], dv0_b, sA[:], 0.0,
                                                  ALU.mult, ALU.add),
                   waits=[("cg", mg_u0)], incs=[("cv", 1)], ms=True)
            w_c = [("cg", ms_xfree[m - 1])] if m >= 1 else []
            msc_c = E.emit("vector",
                   lambda e: e.tensor_tensor_scan(sC[:], dv1_b, xm, 0.0,
                                                  ALU.mult, ALU.add),
                   waits=w_c, incs=[("cv", 1)], ms=True)
            mg_t = E.emit("gpsimd", lambda e: e.tensor_mul(sD[:], sC[:], w1bs[:]),
                          waits=[("cv", msc_c)], incs=[("cg", 1)], ms=True)
            mg_xb = E.emit("gpsimd", lambda e: e.tensor_add(sA[:], xm, biasbs[:]),
                           waits=[("cv", msc_r)], incs=[("cg", 1)], ms=True)
            ms_xfree[m] = mg_xb
            ms_vadd[m] = E.emit("vector", lambda e: e.tensor_add(sC[:], sB[:], sD[:]),
                   waits=[("cg", mg_t)], incs=[("cv", 1)], ms=True)
            wu = [("cv", ms_norm[m - 3])] if m >= 3 else []
            ms_uc[m] = E.emit("vector",
                   lambda e: e.scalar_tensor_tensor(ub, sC[:], 1.0, sA[:],
                                                    ALU.bypass, ALU.add,
                                                    accum_out=stats[:, 0:1]),
                   waits=wu + [("cg", mg_xb)], incs=[("cv", 1)], ms=True)
            mq = E.emit("vector",
                   lambda e: e.scalar_tensor_tensor(sB[:], ub, 1.0, ub,
                                                    ALU.bypass, ALU.mult,
                                                    accum_out=stats[:, 1:2]),
                   incs=[("cv", 1)], ms=True)
            o = 8 + (m % 2) * 4
            mq = E.emit("vector",
                   lambda e, o=o: e.tensor_scalar(stats[:, o:o + 1], stats[:, 0:1],
                                                  -1.0 / D, None, ALU.mult),
                   waits=[("cv", mq)], incs=[("cv", 1)], ms=True)
            mq = E.emit("vector",
                   lambda e: e.tensor_scalar(stats[:, 3:4], stats[:, 1:2],
                                             1.0 / D, EPS, ALU.mult, ALU.add),
                   waits=[("cv", mq)], incs=[("cv", 1)], ms=True)
            mq = E.emit("vector",
                   lambda e, o=o: e.tensor_scalar(stats[:, 4:5], stats[:, o:o + 1],
                                                  stats[:, o:o + 1], None, ALU.mult),
                   waits=[("cv", mq)], incs=[("cv", 1)], ms=True)
            mq = E.emit("vector",
                   lambda e: e.tensor_sub(stats[:, 5:6], stats[:, 3:4], stats[:, 4:5]),
                   waits=[("cv", mq)], incs=[("cv", 1)], ms=True)
            ms_var[m] = E.emit("vector",
                         lambda e, o=o: e.tensor_scalar(stats[:, o + 2:o + 3],
                                                        stats[:, 5:6],
                                                        1e-30, None, ALU.max),
                         waits=[("cv", mq)], incs=[("cv", 1)], ms=True)

        def attn_s1(m):
            o = 8 + (m % 2) * 4
            ms_sqrt[m] = E.emit("scalar",
                         lambda e, o=o: e.activation(stats[:, o + 3:o + 4],
                                                     stats[:, o + 2:o + 3], AF.Sqrt),
                         waits=[("cv", ms_var[m])], incs=[("cs", 1)], ms=True)

        def attn_v2(m):
            o = 8 + (m % 2) * 4
            ub = ubf[:, (m % 3) * D:(m % 3) * D + D]
            h1b = h1nb[:, (m % 2) * D:(m % 2) * D + D]
            msr = E.emit("vector",
                   lambda e, o=o: e.reciprocal(stats[:, o + 1:o + 2],
                                               stats[:, o + 3:o + 4]),
                   waits=[("cs", ms_sqrt[m])], incs=[("cv", 1)], ms=True)
            wh = ([(f"sHN{m % 2}", ms_h1free[m - 2])] if m >= 2 else []) + [("cv", msr)]
            ms_norm[m] = E.emit("vector",
                                lambda e, o=o, h1b=h1b, ub=ub: e.tensor_scalar(
                                    h1b, ub, stats[:, o:o + 1], stats[:, o + 1:o + 2],
                                    ALU.add, ALU.mult),
                                waits=wh, incs=[("cv", 1)], ms=True)

        def attn_s2(m):
            h1b = h1nb[:, (m % 2) * D:(m % 2) * D + D]
            w_tp = [("cv", ms_norm[m])] + \
                   ([(f"sHS{m % 2}", ms_h1tp[m - 2])] if m >= 2 else [])
            mtp = E.emit("scalar",
                   lambda e, m=m, h1b=h1b: e.dma_start_transpose(
                       tbuf[:, (m % 2) * D:(m % 2) * D + D]
                           .rearrange("p (t q) -> p t q", t=KB),
                       h1b),
                   waits=w_tp, incs=[(f"sHT{m % 2}", 16)], ms=True)
            ms_h1tp[m] = E.emit("scalar",
                   lambda e, m=m: e.dma_start(
                       out=h1t_d[:, m * 128:(m + 1) * 128]
                           .rearrange("(k p) q -> p k q", p=128),
                       in_=tbuf[:, (m % 2) * D:(m % 2) * D + D]
                           .rearrange("p (t q) -> p t q", t=KB)),
                   waits=[(f"sHT{m % 2}", mtp)], incs=[(f"sHS{m % 2}", 16)], ms=True)
            ms_h1free[m] = E.emit("scalar",
                                  lambda e, m=m, h1b=h1b: e.dma_start(
                                      out=h1n_d[m * 128:(m + 1) * 128, :], in_=h1b),
                                  incs=[(f"sHN{m % 2}", 16)], ms=True)

        att_v1_next = [0]
        att_fin_next = [0]

        def attn_tick():
            if att_fin_next[0] < att_v1_next[0] and att_fin_next[0] < MT:
                m = att_fin_next[0]
                attn_s1(m)
                attn_v2(m)
                attn_s2(m)
                att_fin_next[0] += 1
            if att_v1_next[0] < MT:
                m = att_v1_next[0]
                attn_v1(m)
                emit_x_dma(m + 2)
                att_v1_next[0] += 1

        ms_w1 = {}
        ms_w2 = {}
        ms_p1 = {}
        ms_p2 = {}
        jobs = []
        w1_ji = {}
        w2_ji = {}
        w3_last_ji = {}
        w3list = [(cc, b) for cc in range(3) for b in range(KB)]
        w3ptr = 0
        for k in range(HBK):
            w1_ji[k] = len(jobs)
            jobs.append((0, w1f_d[k * 128:(k + 1) * 128, :], D,
                         q1t_d[k * 128:(k + 1) * 128, :], False))
            w2_ji[k] = len(jobs)
            jobs.append((1, w2f_d[k * 128:(k + 1) * 128, :], D,
                         q2t_d[k * 128:(k + 1) * 128, :], False))
            if w3ptr < len(w3list):
                cc, b = w3list[w3ptr]
                Fc = 1536 if cc == 2 else 2048
                w3_last_ji[cc] = len(jobs)
                jobs.append((2, w3f_d[b * 128:(b + 1) * 128, cc * 2048:cc * 2048 + Fc],
                             Fc,
                             q3t_d[cc * 2048:cc * 2048 + Fc, b * 128:(b + 1) * 128]
                             .rearrange("(t p) q -> p t q", p=128), True))
                w3ptr += 1
        while w3ptr < len(w3list):
            cc, b = w3list[w3ptr]
            Fc = 1536 if cc == 2 else 2048
            w3_last_ji[cc] = len(jobs)
            jobs.append((2, w3f_d[b * 128:(b + 1) * 128, cc * 2048:cc * 2048 + Fc],
                         Fc,
                         q3t_d[cc * 2048:cc * 2048 + Fc, b * 128:(b + 1) * 128]
                         .rearrange("(t p) q -> p t q", p=128), True))
            w3ptr += 1
        NJ = len(jobs)
        job_hb = {}
        for k in range(HBK):
            job_hb[w1_ji[k]] = k
            job_hb[w2_ji[k]] = k
        ms_in = [None] * NJ
        ms_s1 = [None] * NJ
        ms_v2 = [None] * NJ
        ms_cmb = [None] * NJ
        ms_tp = [None] * NJ
        ms_st = [None] * NJ

        def q_in(ji):
            if ji >= NJ:
                return
            wi, src, F, dst, is3 = jobs[ji]
            pin_s = pin[:, (ji % 3) * D:(ji % 3) * D + F]
            w_in = []
            if ji >= 3:
                w_in += [("cs", ms_s1[ji - 3]), ("cv", ms_v2[ji - 3])]
            ms_in[ji] = E.emit("sync" if ji % 2 == 0 else "gpsimd",
                               lambda e, pin_s=pin_s, src=src: e.dma_start(
                                   out=pin_s, in_=src),
                               waits=w_in, incs=[(f"sP{ji % 3}", 16)], ms=True)

        def q_sgn(ji):
            wi, src, F, dst, is3 = jobs[ji]
            pin_s = pin[:, (ji % 3) * D:(ji % 3) * D + F]
            s1_s = s1b[:, (ji % 2) * D:(ji % 2) * D + F]
            w_s = [(f"sP{ji % 3}", ms_in[ji]), ("cv", ms_th[wi])] + \
                  ([("cv", ms_cmb[ji - 2])] if ji >= 2 else [])
            ms_s1[ji] = E.emit("scalar",
                               lambda e, s1_s=s1_s, pin_s=pin_s, wi=wi: e.sign(
                                   s1_s, pin_s, bias=thc[:, 4 + wi:5 + wi]),
                               waits=w_s, incs=[("cs", 1)], ms=True)

        def q_lt(ji):
            wi, src, F, dst, is3 = jobs[ji]
            pin_s = pin[:, (ji % 3) * D:(ji % 3) * D + F]
            v2_s = v2b[:, (ji % 2) * D:(ji % 2) * D + F]
            w_v = [(f"sP{ji % 3}", ms_in[ji]), ("cv", ms_th[wi])] + \
                  ([("cv", ms_cmb[ji - 2])] if ji >= 2 else [])
            ms_v2[ji] = E.emit("vector",
                               lambda e, v2_s=v2_s, pin_s=pin_s, wi=wi: e.tensor_scalar(
                                   v2_s, pin_s, thc[:, 4 + wi:5 + wi], 0.5,
                                   ALU.is_lt, ALU.subtract),
                               waits=w_v, incs=[("cv", 1)], ms=True)

        def q_cmb(ji):
            if ji < 0 or ji >= NJ:
                return
            wi, src, F, dst, is3 = jobs[ji]
            s1_s = s1b[:, (ji % 2) * D:(ji % 2) * D + F]
            v2_s = v2b[:, (ji % 2) * D:(ji % 2) * D + F]
            qn_s = qnb[:, (ji % 2) * D:(ji % 2) * D + F]
            w_c = [("cs", ms_s1[ji])]
            if ji >= 2:
                tsem, tval = ms_tp[ji - 2]
                w_c.append((tsem, tval))
            ms_cmb[ji] = E.emit("vector",
                                lambda e, qn_s=qn_s, v2_s=v2_s, s1_s=s1_s:
                                e.scalar_tensor_tensor(qn_s, v2_s, -2.0, s1_s,
                                                       ALU.mult, ALU.add),
                                waits=w_c, incs=[("cv", 1)], ms=True)

        def q_tp(ji):
            if ji < 0 or ji >= NJ:
                return
            wi, src, F, dst, is3 = jobs[ji]
            qn_s = qnb[:, (ji % 2) * D:(ji % 2) * D + F]
            nt = F // 128
            if wi == 2:
                qt_s = qtb[:, (ji % 2) * D:(ji % 2) * D + F]
                w_t = [("cv", ms_cmb[ji])]
                if ji >= 2:
                    w_t.append((f"sS{ji % 2}", ms_st[ji - 2]))
                ms_tp[ji] = (f"sT{ji % 2}", E.emit(
                    "sync",
                    lambda e, qt_s=qt_s, qn_s=qn_s, nt=nt: e.dma_start_transpose(
                        qt_s.rearrange("p (t q) -> p t q", t=nt), qn_s),
                    waits=w_t, incs=[(f"sT{ji % 2}", 16)], ms=True))
            else:
                hb = job_hb[ji]
                wt = w1t if wi == 0 else w2t
                sem = f"sW1{hb % 2}" if wi == 0 else f"sW2{hb % 2}"
                dstt = wt[:, (hb % 2) * D:(hb % 2) * D + D]
                w_t = [("cv", ms_cmb[ji])]
                if hb >= 2:
                    pms = ms_p1.get((0, hb - 2)) if wi == 0 else ms_p2.get((0, hb - 2))
                    assert pms is not None, (ji, hb)
                    w_t.append(("ct", pms))
                    prev_ji = w1_ji[hb - 2] if wi == 0 else w2_ji[hb - 2]
                    w_t.append((f"sS{prev_ji % 2}", ms_st[prev_ji]))
                val = E.emit(
                    "sync",
                    lambda e, dstt=dstt, qn_s=qn_s, nt=nt: e.dma_start_transpose(
                        dstt.rearrange("p (t q) -> p t q", t=nt), qn_s),
                    waits=w_t, incs=[(sem, 16)], ms=True)
                ms_tp[ji] = (sem, val)
                if wi == 0:
                    ms_w1[(0, hb)] = val
                else:
                    ms_w2[(0, hb)] = val

        def q_st(ji):
            if ji < 0 or ji >= NJ:
                return
            wi, src, F, dst, is3 = jobs[ji]
            nt = F // 128
            if wi == 2:
                qt_s = qtb[:, (ji % 2) * D:(ji % 2) * D + F]
                src_st = qt_s.rearrange("p (t q) -> p t q", t=nt)
            else:
                hb = job_hb[ji]
                wt = w1t if wi == 0 else w2t
                src_st = wt[:, (hb % 2) * D:(hb % 2) * D + D]
            tsem, tval = ms_tp[ji]
            ms_st[ji] = E.emit("gpsimd",
                               lambda e, dst=dst, src_st=src_st: e.dma_start(
                                   out=dst, in_=src_st),
                               waits=[(tsem, tval)],
                               incs=[(f"sS{ji % 2}", 16)], ms=True)

        job_next = [0]

        def q_tick():
            ji = job_next[0]
            if ji >= NJ + 2:
                return False
            if ji == 0:
                q_in(0)
                q_in(1)
            q_in(ji + 2)
            q_st(ji - 2)
            if ji < NJ:
                q_sgn(ji)
                q_lt(ji)
            q_cmb(ji - 1)
            q_tp(ji - 1)
            job_next[0] += 1
            return True

        def prep_done_waits():
            w = [("cs", ms_s1[NJ - 1]), ("cv", ms_cmb[NJ - 1])]
            w += [(f"sP{i}", E.n[f"sP{i}"]) for i in range(3)]
            w += [(f"sT{i}", E.n[f"sT{i}"]) for i in range(2)]
            w += [(f"sS{i}", E.n[f"sS{i}"]) for i in range(2)]
            return w

        emit_x_dma(0)
        emit_x_dma(1)
        while att_fin_next[0] < MT:
            attn_tick()
        for _ in range(6):
            q_tick()

        attn_waits = ([(f"sHT{i}", E.n[f"sHT{i}"]) for i in range(2)] +
                      [(f"sHS{i}", E.n[f"sHS{i}"]) for i in range(2)] +
                      [(f"sHN{i}", E.n[f"sHN{i}"]) for i in range(2)] +
                      [(f"sX{i}", E.n[f"sX{i}"]) for i in range(2)] +
                      [(f"sG{i}", E.n[f"sG{i}"]) for i in range(2)] +
                      [("cs", ms_gab[NCHUNK - 1])])
        attn_st.close()
        post_st = contextlib.ExitStack()
        hTq = post_st.enter_context(nc.sbuf_tensor("hTq", [128, HBK * QT], BF16))
        y2acc = post_st.enter_context(nc.sbuf_tensor("y2acc", [128, 4 * D], F32))
        w3t = post_st.enter_context(nc.sbuf_tensor("w3t", [128, 2 * G3 * 512], BF16))
        first_post = [True]

        ms_h1q = [None]
        ms_sig = {}
        ms_hT = {}
        ms_w3l = {}
        ms_f3u = {}
        ms_ev = {}
        ms_f3last = {}
        ln2_first = [True]
        ms_ln2free = [None] * 4
        ms_outd = [None] * 4
        ln2_bufs = {}

        def emit_h1q(q):
            w = [("cv", ms_norm[4 * q + 3]),
                 (f"sHS{(4 * q + 3) % 2}", ms_h1tp[4 * q + 3]),
                 (f"sHS{(4 * q + 2) % 2}", ms_h1tp[4 * q + 2])]
            if q > 0:
                w.append(("ct", ms_p2[(q - 1, HBK - 1)]))
            ms_h1q[0] = E.emit(
                "sync",
                lambda e, q=q: e.dma_start(
                    out=h1Th[:].rearrange("p (k n) -> p k n", k=KB),
                    in_=h1t_d[:, q * QT:(q + 1) * QT]
                        .rearrange("(k p) n -> p k n", p=128)),
                waits=w, incs=[("sH1", 16)], ms=True)

        def emit_w_load(q, hb):
            if hb >= HBK:
                return
            j1, j2 = w1_ji[hb], w2_ji[hb]
            assert ms_st[j1] is not None and ms_st[j2] is not None, (hb, j1, j2)
            w_1 = [(f"sS{j1 % 2}", ms_st[j1])]
            w_2x = []
            if q == 1:
                ps1 = w1_ji[HBK - 2 + hb] if hb < 2 else None
                ps2 = w2_ji[HBK - 2 + hb] if hb < 2 else None
                if ps1 is not None:
                    w_1.append((f"sS{ps1 % 2}", ms_st[ps1]))
                    w_2x.append((f"sS{ps2 % 2}", ms_st[ps2]))
            prev = (q, hb - 2) if hb >= 2 else ((q - 1, HBK - 2 + hb) if q > 0 else None)
            if prev in ms_p1:
                w_1.append(("ct", ms_p1[prev]))
            ms_w1[(q, hb)] = E.emit(
                "sync",
                lambda e, hb=hb: e.dma_start(
                    out=w1t[:, (hb % 2) * D:(hb % 2) * D + D],
                    in_=q1t_d[hb * 128:(hb + 1) * 128, :]),
                waits=w_1, incs=[(f"sW1{hb % 2}", 16)], ms=True)
            w_2 = [(f"sS{j2 % 2}", ms_st[j2])] + w_2x
            if prev in ms_p2:
                w_2.append(("ct", ms_p2[prev]))
            ms_w2[(q, hb)] = E.emit(
                "sync",
                lambda e, hb=hb: e.dma_start(
                    out=w2t[:, (hb % 2) * D:(hb % 2) * D + D],
                    in_=q2t_d[hb * 128:(hb + 1) * 128, :]),
                waits=w_2, incs=[(f"sW2{hb % 2}", 16)], ms=True)

        def emit_f12(q, hb):
            wslab1 = w1t[:, (hb % 2) * D:(hb % 2) * D + D]
            wslab2 = w2t[:, (hb % 2) * D:(hb % 2) * D + D]
            w_mm = [(f"sW1{hb % 2}", ms_w1[(q, hb)]), ("sH1", ms_h1q[0])]
            ps1 = px1[:, (hb % 2) * QT:(hb % 2) * QT + QT]
            ps2 = px2[:, (hb % 2) * QT:(hb % 2) * QT + QT]
            prevsig = (q, hb - 2) if hb >= 2 else ((q - 1, HBK - 2 + hb) if q > 0 else None)
            if prevsig in ms_sig:
                w_mm.append(("cs", ms_sig[prevsig]))
            for k in range(KB):
                def mk(e, k=k, wslab=wslab1, ps1=ps1):
                    return e.matmul(ps1, wslab[:, k * 128:(k + 1) * 128],
                                    h1Th[:, k * QT:(k + 1) * QT],
                                    start=(k == 0), stop=(k == KB - 1))
                if k == 0:
                    E.emit("tensor", mk, waits=w_mm)
                elif k == KB - 1:
                    ms_p1[(q, hb)] = E.emit("tensor", mk, incs=[("ct", 1)], ms=True)
                else:
                    E.emit("tensor", mk)
            w_mm2 = [(f"sW2{hb % 2}", ms_w2[(q, hb)])]
            prevhT = (q, hb - 2) if hb >= 2 else ((q - 1, HBK - 2 + hb) if q > 0 else None)
            if prevhT in ms_hT:
                w_mm2.append(("cv", ms_hT[prevhT]))
            for k in range(KB):
                def mk2(e, k=k, wslab=wslab2, ps2=ps2):
                    return e.matmul(ps2, wslab[:, k * 128:(k + 1) * 128],
                                    h1Th[:, k * QT:(k + 1) * QT],
                                    start=(k == 0), stop=(k == KB - 1))
                if k == 0:
                    E.emit("tensor", mk2, waits=w_mm2)
                elif k == KB - 1:
                    ms_p2[(q, hb)] = E.emit("tensor", mk2, incs=[("ct", 1)], ms=True)
                else:
                    E.emit("tensor", mk2)
            w_sig = [("ct", ms_p1[(q, hb)])]
            if prevhT in ms_hT:
                w_sig.append(("cv", ms_hT[prevhT]))
            ms_sig[(q, hb)] = E.emit(
                "scalar",
                lambda e, hb=hb, ps1=ps1: e.activation(
                    sgb[:, (hb % 2) * QT:(hb % 2) * QT + QT],
                    ps1, AF.Sigmoid, scale=0.5),
                waits=w_sig, incs=[("cs", 1)], ms=True)
            w_ht = [("cs", ms_sig[(q, hb)]), ("ct", ms_p2[(q, hb)])]
            prevf3 = (q - 1, hb // G3)
            if prevf3 in ms_f3last:
                w_ht.append(("ct", ms_f3last[prevf3]))
            if first_post[0]:
                first_post[0] = False
                w_ht = attn_waits + w_ht
            ms_hT[(q, hb)] = E.emit(
                "vector",
                lambda e, hb=hb, ps2=ps2: e.tensor_mul(
                    hTq[:, hb * QT:(hb + 1) * QT],
                    sgb[:, (hb % 2) * QT:(hb % 2) * QT + QT], ps2),
                waits=w_ht, incs=[("cv", 1)], ms=True)

        def emit_w3_load(q, g, dq, prev):
            w = []
            if prev in ms_f3u:
                w.append(("ct", ms_f3u[prev]))
            lo_c = (g * G3 * 128) // 2048
            hi_c = ((g + 1) * G3 * 128 - 1) // 2048
            for ccx in range(lo_c, hi_c + 1):
                jid = w3_last_ji[ccx]
                assert ms_st[jid] is not None, (g, dq, ccx, jid)
                w.append((f"sS{jid % 2}", ms_st[jid]))
            ms_w3l[(q, g, dq)] = E.emit(
                "sync",
                lambda e, g=g, dq=dq: e.dma_start(
                    out=w3t[:, (dq % 2) * G3 * 512:(dq % 2) * G3 * 512 + G3 * 512]
                        .rearrange("p (t n) -> p t n", t=G3),
                    in_=q3t_d[g * G3 * 128:(g + 1) * G3 * 128, dq * 512:(dq + 1) * 512]
                        .rearrange("(t p) n -> p t n", p=128)),
                waits=w, incs=[(f"sW3{dq % 2}", 16)], ms=True)

        def emit_f3(q, g):
            unit = 0
            msu = None
            for dq in range(4):
                if dq == 2:
                    emit_w3_load(q, g, 2, (q, g, 0))
                    emit_w3_load(q, g, 3, (q, g, 1))
                for m in range(4):
                    pp = pyA if unit % 2 == 0 else pyB
                    w_u = [("cv", ms_hT[(q, (g + 1) * G3 - 1)]),
                           (f"sW3{dq % 2}", ms_w3l[(q, g, dq)])]
                    prev_ev = None
                    if unit >= 2:
                        prev_ev = ms_ev.get((q, g, unit - 2))
                    elif g >= 1:
                        prev_ev = ms_ev.get((q, g - 1, 14 + unit))
                    elif q > 0:
                        prev_ev = ms_ev.get((q - 1, NG3 - 1, 14 + unit))
                    if prev_ev is not None:
                        w_u.append(("cv", prev_ev))
                    for kk in range(G3):
                        def mk3(e, kk=kk, m=m, dq=dq, g=g, pp=pp):
                            return e.matmul(
                                pp[:],
                                hTq[:, (g * G3 + kk) * QT + m * 128:
                                       (g * G3 + kk) * QT + (m + 1) * 128],
                                w3t[:, (dq % 2) * G3 * 512 + kk * 512:
                                       (dq % 2) * G3 * 512 + kk * 512 + 512],
                                start=(kk == 0), stop=(kk == G3 - 1))
                        if kk == 0:
                            E.emit("tensor", mk3, waits=w_u)
                        elif kk == G3 - 1:
                            msu = E.emit("tensor", mk3, incs=[("ct", 1)], ms=True)
                        else:
                            E.emit("tensor", mk3)
                    ms_f3u[(q, g, dq)] = msu
                    ydst = y2acc[:, m * D + dq * 512: m * D + dq * 512 + 512]
                    w_e = [("ct", msu)]
                    if g == 0:
                        mse = E.emit("vector",
                                     lambda e, ydst=ydst, pp=pp: e.tensor_copy(ydst, pp[:]),
                                     waits=w_e, incs=[("cv", 1)], ms=True)
                    else:
                        mse = E.emit("vector",
                                     lambda e, ydst=ydst, pp=pp: e.scalar_tensor_tensor(
                                         ydst, pp[:], 1.0, ydst, ALU.bypass, ALU.add),
                                     waits=w_e, incs=[("cv", 1)], ms=True)
                    ms_ev[(q, g, unit)] = mse
                    unit += 1
            if g + 1 < NG3:
                emit_w3_load(q, g + 1, 0, (q, g, 2))
                emit_w3_load(q, g + 1, 1, (q, g, 3))
            elif q + 1 < NQ:
                emit_w3_load(q + 1, 0, 0, (q, g, 2))
                emit_w3_load(q + 1, 0, 1, (q, g, 3))
            ms_f3last[(q, g)] = msu

        def alloc_ln2():
            ln2_bufs["uU"] = pin[:, 0:D]
            ln2_bufs["outb"] = pin[:, D:3 * D]
            ln2_bufs["h1nb2"] = s1b[:, 0:2 * D]

        def emit_ln2_m(q, m):
            uU_a = ln2_bufs["uU"]
            h1nb2_a = ln2_bufs["h1nb2"]
            outb_a = ln2_bufs["outb"]
            last_ev = ms_ev[(q, NG3 - 1, 15)]
            if True:
                mg = q * 4 + m
                w_hn = [("cv", ms_ln2free[(m - 2) % 4])] if m >= 2 else []
                if ln2_first[0]:
                    ln2_first[0] = False
                    w_hn = prep_done_waits() + w_hn
                elif m < 2 and q > 0:
                    w_hn = [("cv", ms_ln2free[(m + 2) % 4])] + w_hn
                ms_hnm = E.emit(
                    "sync",
                    lambda e, mg=mg, m=m: e.dma_start(
                        out=h1nb2_a[:, (m % 2) * D:(m % 2) * D + D],
                        in_=h1n_d[mg * 128:(mg + 1) * 128, :]),
                    waits=w_hn, incs=[(f"sHB{m % 2}", 16)], ms=True)
                h1s = h1nb2_a[:, (m % 2) * D:(m % 2) * D + D]
                yv = y2acc[:, m * D:(m + 1) * D]
                ob = outb_a[:, (m % 2) * D:(m % 2) * D + D]
                E.emit("vector",
                       lambda e, h1s=h1s, yv=yv: e.scalar_tensor_tensor(
                           uU_a, h1s, 4.0, yv, ALU.mult, ALU.add,
                           accum_out=st2[:, 0:1]),
                       waits=[(f"sHB{m % 2}", ms_hnm), ("cv", last_ev)],
                       incs=[("cv", 1)])
                w_sq = [(f"sO{m % 2}", ms_outd[(m - 2) % 4])] if m >= 2 else \
                       ([(f"sO{m % 2}", ms_outd[(m + 2) % 4])] if q > 0 else [])
                mq2 = E.emit("vector",
                       lambda e, ob=ob: e.scalar_tensor_tensor(
                           ob, uU_a, 1.0, uU_a, ALU.bypass, ALU.mult,
                           accum_out=st2[:, 1:2]),
                       waits=w_sq, incs=[("cv", 1)], ms=True)
                mq2 = E.emit("vector",
                       lambda e: e.tensor_scalar(st2[:, 2:3], st2[:, 0:1],
                                                 -1.0 / D, None, ALU.mult),
                       waits=[("cv", mq2)], incs=[("cv", 1)], ms=True)
                mq2 = E.emit("vector",
                       lambda e: e.tensor_scalar(st2[:, 3:4], st2[:, 1:2],
                                                 1.0 / D, EPS16, ALU.mult, ALU.add),
                       waits=[("cv", mq2)], incs=[("cv", 1)], ms=True)
                mq2 = E.emit("vector",
                       lambda e: e.tensor_scalar(st2[:, 4:5], st2[:, 2:3],
                                                 st2[:, 2:3], None, ALU.mult),
                       waits=[("cv", mq2)], incs=[("cv", 1)], ms=True)
                mq2 = E.emit("vector",
                       lambda e: e.tensor_sub(st2[:, 5:6], st2[:, 3:4], st2[:, 4:5]),
                       waits=[("cv", mq2)], incs=[("cv", 1)], ms=True)
                msv2 = E.emit("vector",
                              lambda e: e.tensor_scalar(st2[:, 5:6], st2[:, 5:6],
                                                        1e-30, None, ALU.max),
                              waits=[("cv", mq2)], incs=[("cv", 1)], ms=True)
                mss2 = E.emit("scalar",
                              lambda e: e.activation(st2[:, 6:7], st2[:, 5:6], AF.Sqrt),
                              waits=[("cv", msv2)], incs=[("cs", 1)], ms=True)
                msr2 = E.emit("vector", lambda e: e.reciprocal(st2[:, 7:8], st2[:, 6:7]),
                       waits=[("cs", mss2)], incs=[("cv", 1)], ms=True)
                ms_ln2free[m % 4] = E.emit(
                    "vector",
                    lambda e, ob=ob: e.tensor_scalar(ob, uU_a, st2[:, 2:3],
                                                     st2[:, 7:8],
                                                     ALU.add, ALU.mult),
                    waits=[("cv", msr2)], incs=[("cv", 1)], ms=True)
                ms_outd[m % 4] = E.emit(
                    "sync",
                    lambda e, mg=mg, m=m: e.dma_start(
                        out=out_d[mg * 128:(mg + 1) * 128, :],
                        in_=outb_a[:, (m % 2) * D:(m % 2) * D + D]),
                    waits=[("cv", ms_ln2free[m % 4])], incs=[(f"sO{m % 2}", 16)], ms=True)
                if _DEBUG_Y2:
                    E.emit("sync",
                           lambda e, mg=mg, m=m: e.dma_start(
                               out=dbgy2_d[mg * 128:(mg + 1) * 128, :],
                               in_=y2acc[:, m * D:(m + 1) * D]),
                           waits=[("cv", ms_ev[(q, NG3 - 1, 15)])],
                           incs=[("sDBG", 16)], ms=True)

        with post_st:
            alloc_ln2()
            for q in range(NQ):
                emit_h1q(q)
                if q > 0:
                    emit_w_load(q, 0)
                    emit_w_load(q, 1)
                for hb in range(HBK):
                    if q == 0:
                        assert (0, hb) in ms_w1 and (0, hb) in ms_w2, hb
                    emit_f12(q, hb)
                    if q == 0:
                        q_tick()
                        q_tick()
                        q_tick()
                    else:
                        emit_w_load(q, hb + 2)
                    if q > 0 and hb in (4, 14, 24, 34):
                        emit_ln2_m(q - 1, (hb - 4) // 10)
                if q == 0:
                    while job_next[0] < NJ + 2:
                        q_tick()
                if q == 0:
                    emit_w3_load(0, 0, 0, None)
                    emit_w3_load(0, 0, 1, None)
                for g in range(NG3):
                    emit_f3(q, g)
            emit_ln2_m(NQ - 1, 0)
            emit_ln2_m(NQ - 1, 1)
            emit_ln2_m(NQ - 1, 2)
            emit_ln2_m(NQ - 1, 3)

            all_dma = [s for s in sem_names if s.startswith("s")]
            E.emit("sync", lambda e: e.nop(),
                   waits=[(nm, E.n[nm]) for nm in all_dma if E.n[nm] > 0])
            E.run_block(nc)
        prep_st.close()
    return nc


def kernel(L0_attn_x, L0_attn_weight, L0_attn_bias, L0_attn_decay_value,
           L0_ffn_w1, L0_ffn_w2, L0_ffn_w3):
    x = np.ascontiguousarray(np.asarray(L0_attn_x, dtype=np.float32)
                             .reshape(NCORE * TOK, D))
    wgt = np.asarray(L0_attn_weight, dtype=np.float32)
    bias = np.asarray(L0_attn_bias, dtype=np.float32)
    dv = np.asarray(L0_attn_decay_value, dtype=np.float32)
    w1 = np.ascontiguousarray(np.asarray(L0_ffn_w1, dtype=np.float32))
    w2 = np.ascontiguousarray(np.asarray(L0_ffn_w2, dtype=np.float32))
    w3 = np.ascontiguousarray(np.asarray(L0_ffn_w3, dtype=np.float32))

    w0b = np.ascontiguousarray(np.broadcast_to(wgt[0:1, :], (128, D)))
    w1b = np.ascontiguousarray(np.broadcast_to(wgt[1:2, :], (128, D)))
    biasb = np.ascontiguousarray(np.broadcast_to(bias.reshape(1, D), (128, D)))
    dvrep = np.ascontiguousarray(np.broadcast_to(dv.reshape(1, 2), (128, 2)))
    onesrow = np.ones((1, 128), np.float32)
    onescol = np.ones((128, 1), np.float32)
    epsh = np.full((1, 1), 0.5e-8, np.float32)
    wflat = [w1.reshape(-1), w2.reshape(-1), w3.reshape(-1)]

    if "prog" not in _NC_CACHE:
        _NC_CACHE["prog"] = build_program()
    nc = _NC_CACHE["prog"]

    in_maps = []
    for c in range(NCORE):
        wsl = np.stack([wf[c * SLICE:(c + 1) * SLICE] for wf in wflat])
        in_maps.append({
            "x": x[c * TOK:(c + 1) * TOK],
            "w0b": w0b, "w1b": w1b, "biasb": biasb, "dvrep": dvrep,
            "onesrow": onesrow, "onescol": onescol, "epsh": epsh,
            "wsl": np.ascontiguousarray(wsl),
            "w1f": w1, "w2f": w2, "w3f": w3,
        })
    res = run_bass_kernel_spmd(nc, in_maps, core_ids=list(range(NCORE)))
    out = np.concatenate([np.asarray(r["out"]) for r in res.results], axis=0)
    return out.reshape(2, 8192, D)


if __name__ == "__main__":
    print("building program ...")
    build_program()
    print("built ok")
